# version 01
# baseline (speedup 1.0000x reference)
import numpy as np
from contextlib import ExitStack
import concourse.bass as bass
import concourse.mybir as mybir
from concourse.bass_utils import run_bass_kernel_spmd

F32 = mybir.dt.float32
BF16 = mybir.dt.bfloat16
AF = mybir.ActivationFunctionType
ALU = mybir.AluOpType
AX = mybir.AxisListType
NCORES = 8
D = 1024
KD = 8
TOK = 2048
EPS = 1e-5
CHAINED = ("vector", "gpsimd", "scalar")


class Prog:

    def __init__(self, nc):
        self.nc = nc
        self.ops = []
        self.self_sems = {e: nc.alloc_semaphore("self_" + e) for e in CHAINED}
        self.k = {e: 0 for e in CHAINED}
        self.all_sems = list(self.self_sems.values())

    def sem(self, name):
        h = self.nc.alloc_semaphore(name)
        self.all_sems.append(h)
        return h

    def op(self, eng, fn, waits=(), incs=(), setup=False, dma=False):
        waits = [w for w in waits if w is not None]
        sig = None
        chain_wait = None
        if eng in CHAINED and not dma:
            assert not incs, "chained ops cannot carry explicit incs"
            if self.k[eng] > 0:
                chain_wait = (self.self_sems[eng], self.k[eng])
            self.k[eng] += 1
            sig = (self.self_sems[eng], self.k[eng])
            incs = [(self.self_sems[eng], 1)]
        else:
            assert len(incs) <= 1
        self.ops.append((eng, fn, tuple(waits), tuple(incs), setup, chain_wait))
        return sig

    def emit(self, setup_sem=None, setup_val=0):
        nc = self.nc
        with nc.Block() as block:
            for name, meth in (("sync", block.sync), ("gpsimd", block.gpsimd),
                               ("scalar", block.scalar), ("vector", block.vector),
                               ("tensor", block.tensor)):
                def body(eng, name=name):
                    first = True
                    for e, fn, waits, incs, setup, chain_wait in self.ops:
                        if e != name:
                            continue
                        if first and not setup and setup_sem is not None:
                            eng.wait_ge(setup_sem, setup_val)
                        if not setup:
                            first = False
                        if chain_wait is not None:
                            eng.wait_ge(*chain_wait)
                        for (h, v) in waits:
                            if v > 0:
                                eng.wait_ge(h, v)
                        ins = fn(eng)
                        for (h, kk) in incs:
                            ins = ins.then_inc(h, kk)
                meth(body)
        nc.all_engine_barrier()
        nc.clear_and_free_semaphores(self.all_sems)
        nc.all_engine_barrier()


class Consts:

    def __init__(self, nc):
        import ml_dtypes
        self._ident_dram = nc.inline_tensor(
            np.eye(128, dtype=np.float32).astype(ml_dtypes.bfloat16),
            name="ident_bf").ap()
        self.nc = nc

    def materialize(self, p, st, extra_dram=()):
        nc = self.nc
        sbt = lambda name, shp, dt: st.enter_context(nc.sbuf_tensor(name, shp, dt))
        self.ident_bf = sbt("c_ident", [128, 128], BF16)[:]
        self.ones_col = sbt("c_ones_col", [128, 1], F32)[:]
        self.ones_row = sbt("c_ones_row", [1, 128], F32)[:]
        self.s_const = p.sem("s_const")
        n = 0
        p.op("sync", lambda e: e.dma_start(out=self.ident_bf, in_=self._ident_dram),
             incs=[(self.s_const, 16)], setup=True)
        n += 16
        p.op("gpsimd", lambda e: e.memset(self.ones_col, 1.0), setup=True, dma=True,
             incs=[(self.s_const, 1)])
        n += 1
        p.op("gpsimd", lambda e: e.memset(self.ones_row, 1.0), setup=True, dma=True,
             incs=[(self.s_const, 1)])
        n += 1
        self.extra = {}
        for (name, dram_ap, shp, dt) in extra_dram:
            t = sbt("c_" + name, shp, dt)
            self.extra[name] = t
            p.op("sync", lambda e, t=t, dram_ap=dram_ap:
                 e.dma_start(out=t[:], in_=dram_ap),
                 incs=[(self.s_const, 16)], setup=True)
            n += 16
        self.ready_val = n


def emit_ln_fm(p, c, xg, xsq, xln, stats_ps, mb_ps, rb_ps, rowbuf, scr,
               s_mm, mmc, ncols, nch=KD, eps=EPS, pre_waits=(), xln_stride=None,
               xln_off=0):
    W = ncols
    mrow = rowbuf[0:1, 0:W]
    vrow = rowbuf[0:1, W:2 * W]
    rrow = rowbuf[0:1, 2 * W:3 * W]
    sq_sig = p.op("scalar", lambda e: e.activation(xsq[:, 0:nch * W],
                                                   xg[:, 0:nch * W], AF.Square),
                  waits=pre_waits)
    for cc in range(nch):
        p.op("tensor",
             lambda e, cc=cc: e.matmul(stats_ps[0:1, 0:W], c.ones_col,
                                       xg[:, cc * W:(cc + 1) * W],
                                       start=(cc == 0), stop=(cc == nch - 1)),
             waits=(pre_waits if cc == 0 else ()))
    for cc in range(nch):
        p.op("tensor",
             lambda e, cc=cc: e.matmul(stats_ps[32:33, 0:W], c.ones_col,
                                       xsq[:, cc * W:(cc + 1) * W],
                                       start=(cc == 0), stop=(cc == nch - 1)),
             waits=([sq_sig] if cc == 0 else ()),
             incs=([(s_mm, 1)] if cc == nch - 1 else ()))
    mmc["n"] += 1
    stat_w = [(s_mm, mmc["n"])]
    p.op("vector", lambda e: e.tensor_scalar(mrow, stats_ps[0:1, 0:W],
                                             1.0 / (nch * 128.0), None, ALU.mult),
         waits=stat_w)
    p.op("vector", lambda e: e.tensor_mul(vrow, mrow, mrow))
    p.op("vector", lambda e: e.scalar_tensor_tensor(vrow, stats_ps[32:33, 0:W],
                                                    1.0 / (nch * 128.0), vrow,
                                                    ALU.mult, ALU.subtract))
    veps = p.op("vector", lambda e: e.tensor_scalar(vrow, vrow, eps, None, ALU.add))
    p.op("scalar", lambda e: e.activation(rrow, vrow, AF.Ln), waits=[veps])
    rsig = p.op("scalar", lambda e: e.activation(rrow, rrow, AF.Exp, scale=-0.5))
    p.op("tensor", lambda e: e.matmul(mb_ps[:, 0:W], c.ones_row, mrow,
                                      start=True, stop=True),
         waits=[rsig])
    p.op("tensor", lambda e: e.matmul(rb_ps[:, 0:W], c.ones_row, rrow,
                                      start=True, stop=True),
         incs=[(s_mm, 1)])
    mmc["n"] += 1
    bc_w = [(s_mm, mmc["n"])]
    xst = W if xln_stride is None else xln_stride
    last = None
    for cc in range(nch):
        p.op("vector", lambda e, cc=cc: e.tensor_sub(
                scr[:, 0:W], xg[:, cc * W:(cc + 1) * W], mb_ps[:, 0:W]),
             waits=(bc_w if cc == 0 else ()))
        last = p.op("vector", lambda e, cc=cc: e.tensor_mul(
                xln[:, cc * xst + xln_off:cc * xst + xln_off + W], scr[:, 0:W],
                rb_ps[:, 0:W]))
    return last


def emit_wprep_transposed(p, c, specs, stag, sgn, pq, ld_slab, ldc, s_tq, tqc,
                          first_evac_waits=()):
    red_sig = None
    sgn_sig = None
    evac_sig = None
    alpha_sigs = []
    first = True
    for (whbm, nslab, scols, nq4, fdim, wT, alp, use_sign) in specs:
        nqq = nq4 // 4
        for m in range(nslab):
            dw = [red_sig, sgn_sig]
            if first:
                dw += list(first_evac_waits)
            p.op("sync", lambda e, whbm=whbm, m=m, scols=scols:
                 e.dma_start(out=stag[:, 0:scols],
                             in_=whbm[m * 128:(m + 1) * 128, :]),
                 waits=dw, incs=[(ld_slab, 16)])
            ldc["n"] += 16
            ldw = [(ld_slab, ldc["n"])]
            if alp is not None:
                red_sig = p.op("vector", lambda e, m=m, scols=scols, alp=alp:
                               e.tensor_reduce(alp[:, m:m + 1], stag[:, 0:scols],
                                               AX.X, ALU.add,
                                               apply_absolute_value=True),
                               waits=ldw)
                alpha_sigs.append(red_sig)
            tq_w = [(s_tq, tqc["n"])]
            if use_sign:
                sgn_sig = p.op("scalar", lambda e, scols=scols:
                               e.sign(sgn[:, 0:scols], stag[:, 0:scols]),
                               waits=ldw + tq_w)
            else:
                sgn_sig = p.op("scalar", lambda e, scols=scols:
                               e.copy(sgn[:, 0:scols], stag[:, 0:scols]),
                               waits=ldw + tq_w)
            for q in range(nqq):
                for j in range(4):
                    kc = q * 4 + j
                    wts = []
                    if j == 0:
                        wts = [sgn_sig, evac_sig]
                        if first:
                            wts += list(first_evac_waits)
                    p.op("tensor", lambda e, j=j, kc=kc:
                         e.transpose(pq[:, j * 128:(j + 1) * 128],
                                     sgn[:, kc * 128:(kc + 1) * 128], c.ident_bf),
                         waits=wts,
                         incs=([(s_tq, 1)] if j == 3 else []))
                tqc["n"] += 1
                first = False
                evac_sig = p.op("vector", lambda e, q=q, m=m, wT=wT, fdim=fdim:
                                e.tensor_copy(
                                    wT.rearrange("p (kc f) -> p kc f", f=fdim)
                                      [:, q * 4:q * 4 + 4, m * 128:(m + 1) * 128],
                                    pq[:, 0:512].rearrange("p (a f) -> p a f", a=4)),
                                waits=[(s_tq, tqc["n"])])
    return evac_sig, alpha_sigs


EXTL = 2304


def build_l0attn():
    nc = bass.Bass("TRN2", target_bir_lowering=False, debug=False)
    x = nc.dram_tensor("x", [D, EXTL], F32, kind="ExternalInput").ap()
    wq = nc.dram_tensor("wq", [D, D], F32, kind="ExternalInput").ap()
    wk = nc.dram_tensor("wk", [D, D], F32, kind="ExternalInput").ap()
    wv = nc.dram_tensor("wv", [D, D], F32, kind="ExternalInput").ap()
    wo = nc.dram_tensor("wo", [D, D], F32, kind="ExternalInput").ap()
    kinit = nc.dram_tensor("kinit", [128, 8 * 63], F32, kind="ExternalInput").ap()
    vinit = nc.dram_tensor("vinit", [128, 8 * 63], F32, kind="ExternalInput").ap()
    mflag = nc.dram_tensor("mflag", [128, 2], F32, kind="ExternalInput").ap()
    maskc = nc.dram_tensor("maskc", [128, 256], F32, kind="ExternalInput").ap()
    y = nc.dram_tensor("y", [D, TOK], F32, kind="ExternalOutput").ap()
    c = Consts(nc)
    PW = 384
    NPG = 6

    with ExitStack() as st:
        sb = lambda name, shp, dt: st.enter_context(nc.sbuf_tensor(name, shp, dt))
        ps = lambda name, shp, dt: st.enter_context(nc.psum_tensor(name, shp, dt))
        xln_ext = sb("xln_ext", [128, KD * EXTL], BF16)
        wqT = sb("wqT", [128, KD * D], BF16)
        wkT = sb("wkT", [128, KD * D], BF16)
        wvT = sb("wvT", [128, KD * D], BF16)
        stag = sb("stag", [128, 1024], F32)
        sgn = sb("sgn", [128, 1024], BF16)
        kin_s = sb("kin_s", [128, 8 * 63], BF16)
        vin_s = sb("vin_s", [128, 8 * 63], BF16)
        mfl = sb("mfl", [128, 2], F32)
        mk = sb("mk", [128, 256], F32)
        scr = sb("scr", [128, 512], F32)
        rowbuf = sb("rowbuf", [1, 3 * PW], F32)
        q_g = sb("q_g", [128, 2 * EXTL], BF16)
        k_g = sb("k_g", [128, 2 * EXTL], BF16)
        v_g = sb("v_g", [128, 2 * EXTL], BF16)
        v_tm = sb("v_tm", [128, 17 * 256], BF16)
        scf = sb("scf", [128, 256], F32)
        exsc = sb("exsc", [128, 256], F32)
        attn_n = sb("attn_n", [128, 256], BF16)
        attnT = sb("attnT", [128, 256], BF16)
        nmx = sb("nmx", [128, 2], F32)
        smr = sb("smr", [128, 4], F32)
        yt = sb("yt", [128, 2 * 512], F32)
        xr = sb("xr", [128, 2 * 512], F32)
        ao_fm = sb("ao_fm", [128, KD * TOK], BF16)
        stats_ps = ps("stats", [128, 512], F32)
        mb_ps = ps("mb_ps", [128, 512], F32)
        rb_ps = ps("rb_ps", [128, 512], F32)
        mm = [ps("mmA", [128, 512], F32), ps("mmB", [128, 512], F32)]
        sc_ps = ps("scA", [128, 512], F32)
        pq = ps("pqA", [128, 1024], BF16)

        p = Prog(nc)
        c.materialize(p, st)
        ld_slab = p.sem("ld_slab")
        ld_m = p.sem("ld_m")
        ld_msw = p.sem("ld_msw")
        ldx = p.sem("ldx")
        ldr = p.sem("ldr")
        s_tq = p.sem("s_tq")
        s_mm = p.sem("s_mm")
        s_st = [p.sem("s_st0"), p.sem("s_st1")]
        nstp = [0, 0]
        mmc = {"n": 0}
        tqc = {"n": 0}
        ldc = {"n": 0}

        for (eng, sem, dst, src) in (("gpsimd", ld_msw, kin_s, kinit),
                                     ("gpsimd", ld_msw, vin_s, vinit),
                                     ("sync", ld_m, mk, maskc),
                                     ("sync", ld_m, mfl, mflag)):
            p.op(eng, lambda e, dst=dst, src=src: e.dma_start(out=dst[:], in_=src),
                 incs=[(sem, 16)], dma=True)

        nldx = 0
        ln_sig = None
        with nc.sbuf_tensor("xg0", [128, KD * PW], F32, side="right") as xg, \
             nc.sbuf_tensor("xsq0", [128, KD * PW], F32, side="right") as xsq:
            for g in range(NPG):
                t0 = g * PW
                p.op("sync", lambda e, t0=t0:
                     e.dma_start(out=xg.rearrange("p (c t) -> p c t", c=KD),
                                 in_=x[:, t0:t0 + PW]
                                     .rearrange("(c p) t -> c p t", p=128)
                                     .rearrange("c p t -> p c t")),
                     waits=[ln_sig], incs=[(ldx, 16)])
                nldx += 16
                ln_sig = emit_ln_fm(p, c, xg, xsq, xln_ext, stats_ps, mb_ps, rb_ps,
                                    rowbuf, scr, s_mm, mmc, PW,
                                    pre_waits=[(ldx, nldx)],
                                    xln_stride=EXTL, xln_off=t0)

        wp_sig, _ = emit_wprep_transposed(
            p, c, [(wq, 8, D, 8, D, wqT, None, False),
                   (wk, 8, D, 8, D, wkT, None, False),
                   (wv, 8, D, 8, D, wvT, None, False)],
            stag, sgn, pq, ld_slab, ldc, s_tq, tqc)

        SCALE = float(1.0 / np.sqrt(64.0))
        ev_sigs = {}
        aolast = None
        pmm = 0
        for hg in range(4):
            mc0 = 2 * hg
            proj_last = None
            for (wTb, dest) in ((wqT, q_g), (wkT, k_g), (wvT, v_g)):
                for mcl in range(2):
                    mc = mc0 + mcl
                    for g in range(NPG):
                        par = pmm % 2
                        for kc in range(KD):
                            wts = []
                            if kc == 0:
                                wts = [wp_sig, ln_sig, ev_sigs.get(par)]
                            p.op("tensor", lambda e, par=par, mc=mc, g=g, kc=kc,
                                 wTb=wTb:
                                 e.matmul(mm[par][:, 0:PW],
                                          wTb.rearrange("p (kc f) -> p kc f", f=D)
                                             [:, kc, mc * 128:(mc + 1) * 128],
                                          xln_ext[:, kc * EXTL + g * PW:
                                                  kc * EXTL + (g + 1) * PW],
                                          start=(kc == 0), stop=(kc == KD - 1)),
                                 waits=wts,
                                 incs=([(s_mm, 1)] if kc == KD - 1 else []))
                        mmc["n"] += 1
                        pmm += 1
                        ev_sigs[par] = p.op(
                            "scalar", lambda e, par=par, mcl=mcl, g=g, dest=dest:
                            e.activation(dest[:, mcl * EXTL + g * PW:
                                              mcl * EXTL + (g + 1) * PW],
                                         mm[par][:, 0:PW], AF.Copy),
                            waits=[(s_mm, mmc["n"])])
                        proj_last = ev_sigs[par]
            bs = None
            for (dst, ini) in ((k_g, kin_s), (v_g, vin_s)):
                for mcl in range(2):
                    mc = mc0 + mcl
                    p.op("vector", lambda e, ini=ini, mc=mc: e.tensor_scalar(
                            scr[:, 0:63], ini[:, mc * 63:(mc + 1) * 63],
                            mfl[:, 0:1], None, ALU.mult),
                         waits=([proj_last, (ld_msw, 32), (ld_m, 32)]
                                if bs is None else []))
                    bs = p.op("vector", lambda e, dst=dst, mcl=mcl:
                              e.scalar_tensor_tensor(
                                  dst[:, mcl * EXTL:mcl * EXTL + 63],
                                  dst[:, mcl * EXTL:mcl * EXTL + 63],
                                  mfl[:, 1:2], scr[:, 0:63], ALU.mult, ALU.add))
            blend_sig = bs

            vq_sig = None
            for t in range(17):
                for mcl in range(2):
                    wts = []
                    if mcl == 0:
                        wts = [blend_sig, vq_sig]
                    p.op("tensor", lambda e, t=t, mcl=mcl:
                         e.transpose(pq[:, mcl * 128:(mcl + 1) * 128],
                                     v_g[:, mcl * EXTL + 128 * t:
                                          mcl * EXTL + 128 * (t + 1)],
                                     c.ident_bf),
                         waits=wts,
                         incs=([(s_tq, 1)] if mcl == 1 else []))
                tqc["n"] += 1
                vq_sig = p.op("vector", lambda e, t=t:
                              e.tensor_copy(v_tm[:, 256 * t:256 * (t + 1)],
                                            pq[:, 0:256]),
                              waits=[(s_tq, tqc["n"])])

            sc_free = None
            at_free = vq_sig
            o_free = [None, None]
            for ht in range(32):
                t = ht // 2
                odd = ht % 2
                kwin = 128 * t + 64 * odd
                q0 = 63 + 128 * t + 64 * odd
                ra0 = 128 * t + 64 * odd
                for hl in range(4):
                    par = hl % 2
                    qq = hl // 2
                    mcl = hl // 2
                    wts = []
                    if hl == 0:
                        wts = [sc_free, blend_sig]
                    p.op("tensor", lambda e, par=par, qq=qq, mcl=mcl,
                         kwin=kwin, q0=q0:
                         e.matmul(sc_ps[64 * par:64 * par + 64,
                                        qq * 128:(qq + 1) * 128],
                                  q_g[64 * par:64 * par + 64,
                                      mcl * EXTL + q0:mcl * EXTL + q0 + 64],
                                  k_g[64 * par:64 * par + 64,
                                      mcl * EXTL + kwin:mcl * EXTL + kwin + 128],
                                  start=True, stop=True,
                                  tile_position=(64 * par, 64 * par)),
                         waits=wts,
                         incs=([(s_mm, 1)] if hl == 3 else []))
                mmc["n"] += 1
                sc_free = p.op("vector", lambda e: e.scalar_tensor_tensor(
                        scf[:], sc_ps[:, 0:256], SCALE, mk[:], ALU.mult, ALU.add),
                     waits=[(s_mm, mmc["n"])])
                nm_sig = p.op("vector", lambda e: e.tensor_reduce(
                        nmx[:], scf[:].rearrange("p (a t) -> p a t", a=2),
                        AX.X, ALU.max, negate=True))
                ex_sig = None
                for qq in range(2):
                    ex_sig = p.op("scalar", lambda e, qq=qq: e.activation(
                            exsc[:, qq * 128:(qq + 1) * 128],
                            scf[:, qq * 128:(qq + 1) * 128],
                            AF.Exp, bias=nmx[:, qq:qq + 1]),
                         waits=([nm_sig] if qq == 0 else ()))
                p.op("vector", lambda e: e.tensor_reduce(
                        smr[:, 0:2], exsc[:].rearrange("p (a t) -> p a t", a=2),
                        AX.X, ALU.add),
                     waits=[ex_sig])
                p.op("vector", lambda e: e.reciprocal(smr[:, 2:4], smr[:, 0:2]))
                nops = []
                for qq in range(2):
                    if odd == 0:
                        nops.append((qq * 128, qq * 128, 128, qq))
                    else:
                        nops.append((qq * 128, qq * 128 + 64, 64, qq))
                        nops.append((qq * 128 + 64, qq * 128, 64, qq))
                an_sig = None
                for (so, do, wd, qq) in nops:
                    an_sig = p.op("vector", lambda e, so=so, do=do, wd=wd, qq=qq:
                         e.tensor_scalar(attn_n[:, do:do + wd], exsc[:, so:so + wd],
                                         smr[:, 2 + qq:3 + qq], None, ALU.mult))
                for qq in range(2):
                    p.op("tensor", lambda e, qq=qq: e.transpose(
                            pq[:, qq * 128:(qq + 1) * 128],
                            attn_n[:, qq * 128:(qq + 1) * 128],
                            c.ident_bf),
                         waits=([an_sig, at_free] if qq == 0 else ()),
                         incs=([(s_tq, 1)] if qq == 1 else []))
                tqc["n"] += 1
                at_free = p.op("vector", lambda e: e.tensor_copy(
                        attnT[:], pq[:, 0:256]),
                     waits=[(s_tq, tqc["n"])])
                for hl in range(4):
                    par = hl % 2
                    qq = hl // 2
                    wts = []
                    if hl == 0:
                        wts = [at_free, o_free[0], o_free[1]]
                    if odd == 0:
                        p.op("tensor", lambda e, hl=hl, par=par, qq=qq, t=t:
                             e.matmul(mm[0][64 * par:64 * par + 64,
                                            qq * 64:(qq + 1) * 64],
                                      v_tm[:, 256 * t + 64 * hl:
                                           256 * t + 64 * (hl + 1)],
                                      attnT[:, qq * 128 + 64 * par:
                                            qq * 128 + 64 * (par + 1)],
                                      start=True, stop=True),
                             waits=wts,
                             incs=([(s_mm, 1)] if hl == 3 else []))
                    else:
                        p.op("tensor", lambda e, hl=hl, par=par, qq=qq, t=t:
                             e.matmul(mm[0][64 * par:64 * par + 64,
                                            qq * 64:(qq + 1) * 64],
                                      v_tm[64:128, 256 * t + 64 * hl:
                                           256 * t + 64 * (hl + 1)],
                                      attnT[64:128, qq * 128 + 64 * par:
                                            qq * 128 + 64 * (par + 1)],
                                      start=True, stop=True),
                             waits=wts)
                        p.op("tensor", lambda e, hl=hl, par=par, qq=qq, t=t:
                             e.matmul(mm[1][64 * par:64 * par + 64,
                                            qq * 64:(qq + 1) * 64],
                                      v_tm[0:64, 256 * (t + 1) + 64 * hl:
                                           256 * (t + 1) + 64 * (hl + 1)],
                                      attnT[0:64, qq * 128 + 64 * par:
                                            qq * 128 + 64 * (par + 1)],
                                      start=True, stop=True),
                             incs=([(s_mm, 1)] if hl == 3 else []))
                mmc["n"] += 1
                if odd == 1:
                    cpsig = p.op("vector", lambda e: e.tensor_copy(
                            scr[:, 256:512], mm[0][:, 0:256]),
                         waits=[(s_mm, mmc["n"])])
                for par in range(2):
                    if odd == 0:
                        o_free[par] = p.op("vector",
                             lambda e, par=par, ra0=ra0, mc0=mc0:
                             e.tensor_copy(
                                 ao_fm.rearrange("p (cc t) -> p cc t", cc=KD)
                                      [64 * par:64 * par + 64, mc0:mc0 + 2,
                                       ra0:ra0 + 64],
                                 mm[0][64 * par:64 * par + 64, 0:128]
                                 .rearrange("p (a t) -> p a t", a=2)),
                             waits=[(s_mm, mmc["n"])])
                    else:
                        o_free[par] = p.op("vector",
                             lambda e, par=par, ra0=ra0, mc0=mc0:
                             e.tensor_add(
                                 ao_fm.rearrange("p (cc t) -> p cc t", cc=KD)
                                      [64 * par:64 * par + 64, mc0:mc0 + 2,
                                       ra0:ra0 + 64],
                                 scr[64 * par:64 * par + 64, 256:384]
                                 .rearrange("p (a t) -> p a t", a=2),
                                 mm[1][64 * par:64 * par + 64, 0:128]
                                 .rearrange("p (a t) -> p a t", a=2)),
                             waits=[(s_mm, mmc["n"])])
                    aolast = o_free[par]
            ev_sigs = {0: aolast, 1: aolast}

        wo_sig, _ = emit_wprep_transposed(
            p, c, [(wo, 8, D, 8, D, wqT, None, False)], stag, sgn, pq,
            ld_slab, ldc, s_tq, tqc,
            first_evac_waits=[(s_mm, mmc["n"]), at_free])
        EW = 512
        nst = 0
        nldr = 0
        est = {}
        for g in range(4):
            for mc in range(KD):
                par = pmm % 2
                p.op("sync", lambda e, mc=mc, g=g, par=par:
                     e.dma_start(out=xr[:, par * EW:(par + 1) * EW],
                                 in_=x[mc * 128:(mc + 1) * 128,
                                       63 + g * EW:63 + (g + 1) * EW]),
                     waits=[est.get(par)], incs=[(ldr, 16)])
                nldr += 16
                for kc in range(KD):
                    wts = []
                    if kc == 0:
                        wts = [wo_sig, aolast, est.get(par)]
                    p.op("tensor", lambda e, par=par, mc=mc, g=g, kc=kc:
                         e.matmul(mm[par][:, 0:EW],
                                  wqT.rearrange("p (kc f) -> p kc f", f=D)
                                     [:, kc, mc * 128:(mc + 1) * 128],
                                  ao_fm[:, kc * TOK + g * EW:
                                        kc * TOK + (g + 1) * EW],
                                  start=(kc == 0), stop=(kc == KD - 1)),
                         waits=wts,
                         incs=([(s_mm, 1)] if kc == KD - 1 else []))
                mmc["n"] += 1
                epar = nst % 2
                esig = p.op("vector", lambda e, par=par, epar=epar: e.tensor_add(
                        yt[:, epar * EW:(epar + 1) * EW], mm[par][:, 0:EW],
                        xr[:, par * EW:(par + 1) * EW]),
                     waits=[(s_mm, mmc["n"]), (ldr, nldr),
                            (s_st[epar], 16 * nstp[epar])])
                est[par] = esig
                pmm += 1
                p.op("sync", lambda e, epar=epar, mc=mc, g=g: e.dma_start(
                        out=y[mc * 128:(mc + 1) * 128, g * EW:(g + 1) * EW],
                        in_=yt[:, epar * EW:(epar + 1) * EW]),
                     waits=[esig], incs=[(s_st[epar], 16)])
                nstp[epar] += 1
                nst += 1
        p.op("sync", lambda e: e.nop(),
             waits=[(s_st[0], nstp[0] * 16), (s_st[1], nstp[1] * 16)])
        p.emit(c.s_const, c.ready_val)
    return nc


def run_l0attn(x0_full, Wq, Wk, Wv, Wo, last_k_init, last_v_init):
    nc = build_l0attn()
    f32 = lambda a: np.asarray(a, np.float32)
    x0 = f32(x0_full)
    xin = np.zeros((NCORES, EXTL, D), np.float32)
    for i in range(NCORES):
        j, s = i // 2, i % 2
        xin[i, 63:63 + TOK] = x0[j, s * TOK:(s + 1) * TOK]
        if s == 1:
            xin[i, 0:63] = x0[j, TOK - 63:TOK]
    xin_t = np.ascontiguousarray(xin.transpose(0, 2, 1))
    ki = np.ascontiguousarray(f32(last_k_init).reshape(63, D).T
                              .reshape(8, 128, 63).transpose(1, 0, 2)
                              .reshape(128, 504))
    vi = np.ascontiguousarray(f32(last_v_init).reshape(63, D).T
                              .reshape(8, 128, 63).transpose(1, 0, 2)
                              .reshape(128, 504))
    maskc = np.full((128, 256), -30000.0, np.float32)
    for r in range(64):
        for qq in range(2):
            maskc[r, qq * 128 + r:qq * 128 + r + 64] = 0.0
            maskc[64 + r, qq * 128 + r:qq * 128 + r + 64] = 0.0
    in_maps = []
    for i in range(NCORES):
        s = i % 2
        mfl = np.zeros((128, 2), np.float32)
        mfl[:, 0] = 1.0 if s == 0 else 0.0
        mfl[:, 1] = 0.0 if s == 0 else 1.0
        in_maps.append({"x": xin_t[i], "wq": f32(Wq), "wk": f32(Wk),
                        "wv": f32(Wv), "wo": f32(Wo), "kinit": ki, "vinit": vi,
                        "mflag": mfl, "maskc": maskc})
    res = run_bass_kernel_spmd(nc, in_maps, core_ids=list(range(NCORES)))
    yv = np.stack([r["y"] for r in res.results])
    return np.ascontiguousarray(yv.transpose(0, 2, 1)).reshape(4, 4096, D)


def build_l0ffn():
    import ml_dtypes
    nc = bass.Bass("TRN2", target_bir_lowering=False, debug=False)
    x = nc.dram_tensor("x", [D, TOK], F32, kind="ExternalInput").ap()
    w1 = nc.dram_tensor("w1", [D, 4096], F32, kind="ExternalInput").ap()
    w2 = nc.dram_tensor("w2", [4096, D], F32, kind="ExternalInput").ap()
    b1t = nc.dram_tensor("b1t", [128, 32], F32, kind="ExternalInput").ap()
    b2t = nc.dram_tensor("b2t", [128, 8], F32, kind="ExternalInput").ap()
    a1h = nc.dram_tensor("a1h", [128, 8 * 24], F32, kind="ExternalInput").ap()
    b1c = nc.dram_tensor("b1c", [24, 4096], F32, kind="ExternalInput").ap()
    r1h = nc.dram_tensor("r1h", [128, 8 * 3], F32, kind="ExternalInput").ap()
    a2h = nc.dram_tensor("a2h", [128, 32 * 24], F32, kind="ExternalInput").ap()
    b2c = nc.dram_tensor("b2c", [24, D], F32, kind="ExternalInput").ap()
    r2h = nc.dram_tensor("r2h", [128, 32 * 3], F32, kind="ExternalInput").ap()
    y = nc.dram_tensor("y", [D, TOK], F32, kind="ExternalOutput").ap()
    c = Consts(nc)
    e24_np = np.zeros((3, 24), np.float32)
    for e_ in range(3):
        e24_np[e_, e_ * 8:(e_ + 1) * 8] = 1.0
    e24_dram = nc.inline_tensor(e24_np.astype(ml_dtypes.bfloat16), name="e24").ap()
    H = 4096
    MH = 32
    NG2 = 8
    W = 256

    with ExitStack() as st:
        sb = lambda name, shp, dt: st.enter_context(nc.sbuf_tensor(name, shp, dt))
        ps = lambda name, shp, dt: st.enter_context(nc.psum_tensor(name, shp, dt))
        w1s = sb("w1s", [128, KD * H], BF16)
        w2s = sb("w2s", [128, MH * D], BF16)
        a1s = sb("a1s", [128, 8 * 24], BF16)
        b1cs = sb("b1cs", [24, H], BF16)
        r1s = sb("r1s", [128, 8 * 3], BF16)
        a2s = sb("a2s", [128, 32 * 24], BF16)
        b2cs = sb("b2cs", [24, D], BF16)
        r2s = sb("r2s", [128, 32 * 3], BF16)
        b1s = sb("b1s", [128, 32], F32)
        b2s = sb("b2s", [128, 8], F32)
        xg = sb("xg", [128, KD * W], F32)
        xsq = sb("xsq", [128, KD * W], F32)
        xln = sb("xln", [128, KD * W], BF16)
        hg = sb("hg", [128, MH * W], BF16)
        z_sb = sb("z_sb", [24, W], BF16)
        zg_sb = sb("zg_sb", [24, W], BF16)
        gex = sb("gex", [128, 8], F32)
        gtm = sb("gtm", [128, 16], BF16)
        gfm = sb("gfm", [3, W], BF16)
        negm = sb("negm", [128, 2], F32)
        srec = sb("srec", [128, 4], F32)
        scr = sb("scr", [128, W], F32)
        tep = sb("tep", [128, 2 * W], F32)
        yt = sb("yt", [128, 2 * W], F32)
        rowbuf = sb("rowbuf", [1, 3 * W], F32)
        stats_ps = ps("stats", [128, 512], F32)
        mbrb = ps("mbrb", [128, 512], F32)
        mm = [ps("mmA", [128, 512], F32), ps("mmB", [128, 512], F32)]
        z_ps = ps("z_ps", [128, 512], F32)
        g_ps = ps("g_ps", [128, 512], F32)
        gt_ps = ps("gt_ps", [128, 1024], BF16)
        ge_ps = ps("ge_ps", [128, 512], F32)

        p = Prog(nc)
        c.materialize(p, st, extra_dram=[("e24", e24_dram, [3, 24], BF16)])
        e24sb = c.extra["e24"]
        p.op("gpsimd", lambda e: e.memset(gtm[:], 0.0), setup=True, dma=True,
             incs=[(c.s_const, 1)])
        c.ready_val += 1
        ld_w = p.sem("ld_w")
        ldx = p.sem("ldx")
        s_mm = p.sem("s_mm")
        s_st = [p.sem("s_st0"), p.sem("s_st1")]
        nstp = [0, 0]
        mmc = {"n": 0}

        nw = 0
        for (dst, src, nkc, fdim) in ((w1s, w1, KD, H), (w2s, w2, MH, D)):
            for kc in range(nkc):
                p.op("gpsimd", lambda e, dst=dst, src=src, kc=kc, fdim=fdim:
                     e.dma_start(out=dst[:, kc * fdim:(kc + 1) * fdim],
                                 in_=src[kc * 128:(kc + 1) * 128, :]),
                     incs=[(ld_w, 16)], dma=True)
                nw += 16
        for (dst, src) in ((a1s, a1h), (b1cs, b1c), (r1s, r1h), (a2s, a2h),
                           (b2cs, b2c), (r2s, r2h), (b1s, b1t), (b2s, b2t)):
            p.op("gpsimd", lambda e, dst=dst, src=src:
                 e.dma_start(out=dst[:], in_=src),
                 incs=[(ld_w, 16)], dma=True)
            nw += 16
        WRDY = nw

        state = {"pmm": 0, "nst": 0, "nldx": 0}
        ev_sigs = {}
        add_sigs = []

        def mole_layer(tg, src_ready, src_sb, nkc, acat, rcat, bcat, wmain, fdim,
                       nm, bias_sb, is_last, hg_free):
            for kc in range(nkc):
                wts = []
                if kc == 0:
                    wts = [src_ready]
                    if tg == 0 and src_sb is xln:
                        wts.append((ld_w, WRDY))
                p.op("tensor", lambda e, kc=kc: e.matmul(
                        z_ps[0:24, 0:W], acat[:, kc * 24:(kc + 1) * 24],
                        src_sb[:, kc * W:(kc + 1) * W],
                        start=(kc == 0), stop=(kc == nkc - 1)),
                     waits=wts)
            for i in range(2):
                for kc in range(nkc):
                    p.op("tensor", lambda e, i=i, kc=kc: e.matmul(
                            g_ps[:, i * 4:i * 4 + 3],
                            src_sb[:, kc * W + i * 128:kc * W + (i + 1) * 128],
                            rcat[:, kc * 3:(kc + 1) * 3],
                            start=(kc == 0), stop=(kc == nkc - 1)),
                         incs=([(s_mm, 1)] if (i == 1 and kc == nkc - 1) else []))
            mmc["n"] += 1
            zg_w = [(s_mm, mmc["n"])]
            zev = p.op("vector", lambda e: e.tensor_copy(z_sb[:], z_ps[0:24, 0:W]),
                       waits=zg_w)
            p.op("vector", lambda e: e.tensor_reduce(negm[:, 0:1], g_ps[:, 0:3],
                                                     AX.X, ALU.max, negate=True))
            nm_sig = p.op("vector", lambda e: e.tensor_reduce(
                    negm[:, 1:2], g_ps[:, 4:7], AX.X, ALU.max, negate=True))
            p.op("scalar", lambda e: e.activation(gex[:, 0:3], g_ps[:, 0:3],
                                                  AF.Exp, bias=negm[:, 0:1]),
                 waits=[nm_sig])
            ex_sig = p.op("scalar", lambda e: e.activation(
                    gex[:, 4:7], g_ps[:, 4:7], AF.Exp, bias=negm[:, 1:2]))
            p.op("vector", lambda e: e.tensor_reduce(
                    srec[:, 0:1], gex[:, 0:3], AX.X, ALU.add), waits=[ex_sig])
            p.op("vector", lambda e: e.tensor_reduce(
                    srec[:, 1:2], gex[:, 4:7], AX.X, ALU.add))
            p.op("vector", lambda e: e.reciprocal(srec[:, 2:4], srec[:, 0:2]))
            gt_sig = None
            for i in range(2):
                gt_sig = p.op("vector", lambda e, i=i: e.tensor_scalar(
                        gtm[:, i * 8:i * 8 + 3], gex[:, i * 4:i * 4 + 3],
                        srec[:, 2 + i:3 + i], None, ALU.mult))
            for i in range(2):
                p.op("tensor", lambda e, i=i: e.transpose(
                        gt_ps[0:8, i * 128:(i + 1) * 128],
                        gtm[:, i * 8:(i + 1) * 8], c.ident_bf),
                     waits=([gt_sig] if i == 0 else ()),
                     incs=([(s_mm, 1)] if i == 1 else []))
            mmc["n"] += 1
            gf_sig = p.op("vector", lambda e: e.tensor_copy(
                    gfm[:], gt_ps[0:3, 0:W]), waits=[(s_mm, mmc["n"])])
            p.op("tensor", lambda e: e.matmul(ge_ps[0:24, 0:W], e24sb[:], gfm[:],
                                              start=True, stop=True),
                 waits=[gf_sig], incs=[(s_mm, 1)])
            mmc["n"] += 1
            zg_sig = p.op("vector", lambda e: e.scalar_tensor_tensor(
                    zg_sb[:], ge_ps[0:24, 0:W], 0.125, z_sb[:],
                    ALU.mult, ALU.mult),
                 waits=[(s_mm, mmc["n"]), zev])
            last_ep = None
            for m in range(nm):
                par = state["pmm"] % 2
                for kc in range(nkc):
                    wts = []
                    if kc == 0:
                        wts = [zg_sig, ev_sigs.get(par)]
                        if m == 0:
                            wts.append(hg_free)
                    p.op("tensor", lambda e, par=par, m=m, kc=kc: e.matmul(
                            mm[par][:, 0:W],
                            wmain.rearrange("p (kc f) -> p kc f", f=fdim)
                                 [:, kc, m * 128:(m + 1) * 128],
                            src_sb[:, kc * W:(kc + 1) * W],
                            start=(kc == 0), stop=False),
                         waits=wts)
                p.op("tensor", lambda e, par=par, m=m: e.matmul(
                        mm[par][:, 0:W], bcat[:, m * 128:(m + 1) * 128], zg_sb[:],
                        start=False, stop=True),
                     incs=[(s_mm, 1)])
                mmc["n"] += 1
                state["pmm"] += 1
                if not is_last:
                    ev_sigs[par] = p.op("scalar", lambda e, par=par, m=m:
                         e.activation(hg[:, m * W:(m + 1) * W], mm[par][:, 0:W],
                                      AF.Relu, bias=bias_sb[:, m:m + 1]),
                         waits=[(s_mm, mmc["n"])])
                    last_ep = ev_sigs[par]
                else:
                    nst = state["nst"]
                    epar = nst % 2
                    tep_guard = [add_sigs[nst - 2]] if nst >= 2 else []
                    ev_sigs[par] = p.op("scalar", lambda e, par=par, epar=epar,
                                        m=m:
                         e.activation(tep[:, epar * W:(epar + 1) * W],
                                      mm[par][:, 0:W], AF.Identity,
                                      bias=bias_sb[:, m:m + 1]),
                         waits=[(s_mm, mmc["n"])] + tep_guard)
                    asig = p.op("vector", lambda e, epar=epar, m=m: e.tensor_add(
                            yt[:, epar * W:(epar + 1) * W],
                            tep[:, epar * W:(epar + 1) * W],
                            xg[:, m * W:(m + 1) * W]),
                         waits=[ev_sigs[par], (s_st[epar], 16 * nstp[epar])])
                    add_sigs.append(asig)
                    p.op("sync", lambda e, epar=epar, m=m, tg=tg: e.dma_start(
                            out=y[m * 128:(m + 1) * 128, tg * W:(tg + 1) * W],
                            in_=yt[:, epar * W:(epar + 1) * W]),
                         waits=[asig], incs=[(s_st[epar], 16)])
                    nstp[epar] += 1
                    state["nst"] += 1
                    last_ep = asig
            return last_ep

        ln_sig = None
        hg_free = None
        for tg in range(NG2):
            t0 = tg * W
            xg_guard = add_sigs[-1] if add_sigs else None
            p.op("sync", lambda e, t0=t0:
                 e.dma_start(out=xg.rearrange("p (c t) -> p c t", c=KD),
                             in_=x[:, t0:t0 + W]
                                 .rearrange("(c p) t -> c p t", p=128)
                                 .rearrange("c p t -> p c t")),
                 waits=[xg_guard, ln_sig], incs=[(ldx, 16)])
            state["nldx"] += 16
            ln_sig = emit_ln_fm(p, c, xg, xsq, xln, stats_ps, mbrb[:, 0:W],
                                mbrb[:, W:2 * W], rowbuf, scr, s_mm, mmc, W,
                                pre_waits=[(ldx, state["nldx"])])
            evA = mole_layer(tg, ln_sig, xln, KD, a1s, r1s, b1cs, w1s, H, MH,
                             b1s, False, hg_free)
            hg_free = mole_layer(tg, evA, hg, MH, a2s, r2s, b2cs, w2s, D, KD,
                                 b2s, True, None)
        p.op("sync", lambda e: e.nop(),
             waits=[(s_st[0], nstp[0] * 16), (s_st[1], nstp[1] * 16)])
        p.emit(c.s_const, c.ready_val)
    return nc


def run_l0ffn(x1_full, W1, b1, A1, B1, R1, W2, b2, A2, B2, R2):
    nc = build_l0ffn()
    xt = np.ascontiguousarray(
        x1_full.reshape(NCORES, TOK, D).transpose(0, 2, 1)).astype(np.float32)
    f32 = lambda a: np.asarray(a, np.float32)
    a1hh = np.ascontiguousarray(
        f32(A1).transpose(1, 0, 2).reshape(8, 128, 24).transpose(1, 0, 2)
        .reshape(128, 8 * 24))
    a2hh = np.ascontiguousarray(
        f32(A2).transpose(1, 0, 2).reshape(32, 128, 24).transpose(1, 0, 2)
        .reshape(128, 32 * 24))
    b1cc = np.ascontiguousarray(f32(B1).reshape(24, 4096))
    b2cc = np.ascontiguousarray(f32(B2).reshape(24, D))
    r1hh = np.ascontiguousarray(
        f32(R1).reshape(8, 128, 3).transpose(1, 0, 2).reshape(128, 24))
    r2hh = np.ascontiguousarray(
        f32(R2).reshape(32, 128, 3).transpose(1, 0, 2).reshape(128, 96))
    b1tt = np.ascontiguousarray(f32(b1).reshape(32, 128).T)
    b2tt = np.ascontiguousarray(f32(b2).reshape(8, 128).T)
    in_maps = [{"x": xt[i], "w1": f32(W1), "w2": f32(W2), "b1t": b1tt,
                "b2t": b2tt, "a1h": a1hh, "b1c": b1cc, "r1h": r1hh,
                "a2h": a2hh, "b2c": b2cc, "r2h": r2hh}
               for i in range(NCORES)]
    res = run_bass_kernel_spmd(nc, in_maps, core_ids=list(range(NCORES)))
    yv = np.stack([r["y"] for r in res.results])
    return np.ascontiguousarray(yv.transpose(0, 2, 1)).reshape(4, 4096, D)


def build_l1ffn():
    nc = bass.Bass("TRN2", target_bir_lowering=False, debug=False)
    x = nc.dram_tensor("x", [D, TOK], F32, kind="ExternalInput").ap()
    w1 = nc.dram_tensor("w1", [4096, D], F32, kind="ExternalInput").ap()
    b1t = nc.dram_tensor("b1t", [128, 32], F32, kind="ExternalInput").ap()
    w2 = nc.dram_tensor("w2", [D, 4096], F32, kind="ExternalInput").ap()
    b2t = nc.dram_tensor("b2t", [128, 8], F32, kind="ExternalInput").ap()
    y = nc.dram_tensor("y", [D, TOK], F32, kind="ExternalOutput").ap()
    c = Consts(nc)
    H = 4096
    MH = 32
    NG2 = 8
    W = 256

    with ExitStack() as st:
        sb = lambda name, shp, dt: st.enter_context(nc.sbuf_tensor(name, shp, dt))
        ps = lambda name, shp, dt: st.enter_context(nc.psum_tensor(name, shp, dt))
        w1T = sb("w1T", [128, KD * H], BF16)
        w2T = sb("w2T", [128, MH * D], BF16)
        stag = sb("stag", [128, 4096], F32)
        sgn = sb("sgn", [128, 4096], BF16)
        alpha1 = sb("alpha1", [128, 32], F32)
        alpha2 = sb("alpha2", [128, 8], F32)
        b1s = sb("b1s", [128, 32], F32)
        b2s = sb("b2s", [128, 8], F32)
        xg = sb("xg", [128, KD * W], F32)
        xsq = sb("xsq", [128, KD * W], F32)
        xln = sb("xln", [128, KD * W], BF16)
        hg = sb("hg", [128, MH * W], BF16)
        scr = sb("scr", [128, W], F32)
        tep = sb("tep", [128, 2 * W], F32)
        yt = sb("yt", [128, 2 * W], F32)
        rowbuf = sb("rowbuf", [1, 3 * W], F32)
        pq = ps("pq0", [128, 1024], BF16)
        stats_ps = ps("stats", [128, 512], F32)
        mbrb = ps("mbrb", [128, 512], F32)
        mm = [ps("mmA", [128, 512], F32), ps("mmB", [128, 512], F32)]

        p = Prog(nc)
        c.materialize(p, st)
        ld_slab = p.sem("ld_slab")
        ld_b = p.sem("ld_b")
        ldx = p.sem("ldx")
        s_tq = p.sem("s_tq")
        s_mm = p.sem("s_mm")
        s_st = [p.sem("s_st0"), p.sem("s_st1")]
        nstp = [0, 0]
        mmc = {"n": 0}
        tqc = {"n": 0}
        ldc = {"n": 0}

        p.op("sync", lambda e: e.dma_start(out=b1s[:], in_=b1t), incs=[(ld_b, 16)])
        p.op("sync", lambda e: e.dma_start(out=b2s[:], in_=b2t), incs=[(ld_b, 16)])

        specs = [(w1, 32, D, 8, H, w1T, alpha1, True),
                 (w2, 8, 4096, 32, D, w2T, alpha2, True)]
        wp_sig, alpha_sigs = emit_wprep_transposed(
            p, c, specs, stag, sgn, pq, ld_slab, ldc, s_tq, tqc)
        p.op("vector", lambda e: e.tensor_scalar(alpha1[:], alpha1[:], 1.0 / D,
                                                 None, ALU.mult),
             waits=[(ld_b, 32), alpha_sigs[-1]])
        al_sig = p.op("vector", lambda e: e.tensor_scalar(
                alpha2[:], alpha2[:], 1.0 / 4096.0, None, ALU.mult))

        ev_sigs = {}
        add_sigs = []
        ln_sig = None
        hg_free = None
        pmm = 0
        nst = 0
        nldx = 0
        for tg in range(NG2):
            t0 = tg * W
            xg_guard = add_sigs[-1] if add_sigs else None
            p.op("sync", lambda e, t0=t0:
                 e.dma_start(out=xg.rearrange("p (c t) -> p c t", c=KD),
                             in_=x[:, t0:t0 + W]
                                 .rearrange("(c p) t -> c p t", p=128)
                                 .rearrange("c p t -> p c t")),
                 waits=[xg_guard, ln_sig], incs=[(ldx, 16)])
            nldx += 16
            ln_sig = emit_ln_fm(p, c, xg, xsq, xln, stats_ps, mbrb[:, 0:W],
                                mbrb[:, W:2 * W], rowbuf, scr, s_mm, mmc, W,
                                pre_waits=[(ldx, nldx)])
            last_gelu = None
            for m in range(MH):
                par = pmm % 2
                for kc in range(KD):
                    wts = []
                    if kc == 0:
                        wts = [ln_sig, ev_sigs.get(par)]
                        if tg == 0 and m == 0:
                            wts.append(wp_sig)
                        if m == 0:
                            wts.append(hg_free)
                    p.op("tensor", lambda e, par=par, m=m, kc=kc:
                         e.matmul(mm[par][:, 0:W],
                                  w1T.rearrange("p (kc f) -> p kc f", f=H)
                                     [:, kc, m * 128:(m + 1) * 128],
                                  xln[:, kc * W:(kc + 1) * W],
                                  start=(kc == 0), stop=(kc == KD - 1)),
                         waits=wts, incs=([(s_mm, 1)] if kc == KD - 1 else []))
                mmc["n"] += 1
                pmm += 1
                ev_sigs[par] = p.op("scalar", lambda e, par=par, m=m:
                     e.activation(hg[:, m * W:(m + 1) * W], mm[par][:, 0:W],
                                  AF.Gelu, bias=b1s[:, m:m + 1],
                                  scale=alpha1[:, m:m + 1]),
                     waits=[(s_mm, mmc["n"])] +
                           ([al_sig] if (tg == 0 and m == 0) else []))
                last_gelu = ev_sigs[par]
            for md in range(KD):
                par = pmm % 2
                for kc in range(MH):
                    wts = []
                    if kc == 0:
                        wts = [ev_sigs.get(par)]
                        if md == 0:
                            wts.append(last_gelu)
                    p.op("tensor", lambda e, par=par, md=md, kc=kc:
                         e.matmul(mm[par][:, 0:W],
                                  w2T.rearrange("p (kc f) -> p kc f", f=D)
                                     [:, kc, md * 128:(md + 1) * 128],
                                  hg[:, kc * W:(kc + 1) * W],
                                  start=(kc == 0), stop=(kc == MH - 1)),
                         waits=wts, incs=([(s_mm, 1)] if kc == MH - 1 else []))
                mmc["n"] += 1
                pmm += 1
                epar = nst % 2
                tep_guard = [add_sigs[nst - 2]] if nst >= 2 else []
                ev_sigs[par] = p.op("scalar", lambda e, par=par, epar=epar, md=md:
                     e.activation(tep[:, epar * W:(epar + 1) * W],
                                  mm[par][:, 0:W], AF.Identity,
                                  bias=b2s[:, md:md + 1],
                                  scale=alpha2[:, md:md + 1]),
                     waits=[(s_mm, mmc["n"])] + tep_guard)
                asig = p.op("vector", lambda e, epar=epar, md=md: e.tensor_add(
                        yt[:, epar * W:(epar + 1) * W],
                        tep[:, epar * W:(epar + 1) * W],
                        xg[:, md * W:(md + 1) * W]),
                     waits=[ev_sigs[par], (s_st[epar], 16 * nstp[epar])])
                add_sigs.append(asig)
                p.op("sync", lambda e, epar=epar, md=md, t0=t0: e.dma_start(
                        out=y[md * 128:(md + 1) * 128, t0:t0 + W],
                        in_=yt[:, epar * W:(epar + 1) * W]),
                     waits=[asig], incs=[(s_st[epar], 16)])
                nstp[epar] += 1
                nst += 1
                hg_free = asig
        p.op("sync", lambda e: e.nop(),
             waits=[(s_st[0], nstp[0] * 16), (s_st[1], nstp[1] * 16)])
        p.emit(c.s_const, c.ready_val)
    return nc


def run_l1ffn(x3_full, w1, b1, w2, b2):
    nc = build_l1ffn()
    xt = np.ascontiguousarray(
        x3_full.reshape(NCORES, TOK, D).transpose(0, 2, 1)).astype(np.float32)
    b1tt = np.ascontiguousarray(np.asarray(b1, np.float32).reshape(32, 128).T)
    b2tt = np.ascontiguousarray(np.asarray(b2, np.float32).reshape(8, 128).T)
    in_maps = [{"x": xt[i], "w1": np.asarray(w1, np.float32), "b1t": b1tt,
                "w2": np.asarray(w2, np.float32), "b2t": b2tt}
               for i in range(NCORES)]
    res = run_bass_kernel_spmd(nc, in_maps, core_ids=list(range(NCORES)))
    yv = np.stack([r["y"] for r in res.results])
    return np.ascontiguousarray(yv.transpose(0, 2, 1)).reshape(4, 4096, D)



EXT3 = 2304


def build_l1front():
    nc = bass.Bass("TRN2", target_bir_lowering=False, debug=False)
    x = nc.dram_tensor("x", [D, EXT3], F32, kind="ExternalInput").ap()
    wqkv = [nc.dram_tensor(n, [D, D], F32, kind="ExternalInput").ap()
            for n in ("wq", "wk", "wv")]
    convs = nc.dram_tensor("convs", [128, 8 * 12], F32, kind="ExternalInput").ap()
    wsm = nc.dram_tensor("wsm", [D, 96], F32, kind="ExternalInput").ap()
    outs = [nc.dram_tensor(n, [D, TOK], BF16, kind="ExternalOutput").ap()
            for n in ("qc", "kc", "vc")]
    bsg = nc.dram_tensor("bsg", [8, TOK], F32, kind="ExternalOutput").ap()
    gout = nc.dram_tensor("gout", [4, TOK], F32, kind="ExternalOutput").ap()
    hout = nc.dram_tensor("hout", [4, TOK], F32, kind="ExternalOutput").ap()
    c = Consts(nc)
    PW = 384
    NPG = 6

    with ExitStack() as st:
        sb = lambda name, shp, dt: st.enter_context(nc.sbuf_tensor(name, shp, dt))
        ps = lambda name, shp, dt: st.enter_context(nc.psum_tensor(name, shp, dt))
        xln_ext = sb("xln_ext", [128, KD * EXT3], BF16)
        w_sb = sb("w_sb", [128, KD * D], BF16)
        conv_s = sb("conv_s", [128, 8 * 12], F32)
        wsm_s = sb("wsm_s", [128, 8 * 96], BF16)
        praw = sb("praw", [128, 2 * EXT3], BF16)
        cacc_v = sb("cacc_v", [128, TOK], F32)
        cacc_g = sb("cacc_g", [128, TOK], F32)
        pout1 = sb("pout1", [128, KD * TOK], BF16)
        pouts = [pout1, pout1, pout1]
        bsg_sb = sb("bsg_sb", [8, TOK], F32)
        g_sb = sb("g_sb", [4, TOK], F32)
        h_sb = sb("h_sb", [4, TOK], F32)
        xg = st.enter_context(nc.sbuf_tensor("xg", [128, KD * PW], F32,
                                             side="right"))
        xsq = st.enter_context(nc.sbuf_tensor("xsq", [128, KD * PW], F32,
                                              side="right"))
        scr = sb("scr", [128, 512], F32)
        rowbuf = sb("rowbuf", [1, 3 * PW], F32)
        stats_ps = ps("stats", [128, 512], F32)
        mb_ps = ps("mb_ps", [128, 512], F32)
        rb_ps = ps("rb_ps", [128, 512], F32)
        mm = [ps("mmA", [128, 512], F32), ps("mmB", [128, 512], F32)]
        sm_ps = ps("sm_ps", [128, 512], F32)

        p = Prog(nc)
        c.materialize(p, st)
        ld_w = p.sem("ld_w")
        ld_w2 = p.sem("ld_w2")
        ldx = p.sem("ldx")
        s_mm = p.sem("s_mm")
        s_st = p.sem("s_st")
        mmc = {"n": 0}
        nst = 0

        nw = 0
        p.op("gpsimd", lambda e: e.dma_start(out=conv_s[:], in_=convs),
             incs=[(ld_w, 16)], dma=True)
        nw += 16
        for kcx in range(KD):
            p.op("gpsimd", lambda e, kcx=kcx: e.dma_start(
                    out=wsm_s[:, kcx * 96:(kcx + 1) * 96],
                    in_=wsm[kcx * 128:(kcx + 1) * 128, :]),
                 incs=[(ld_w, 16)], dma=True)
            nw += 16
        WRDY0 = nw

        nldx = 0
        ln_sig = None
        for g in range(NPG):
            t0 = g * PW
            p.op("sync", lambda e, t0=t0:
                 e.dma_start(out=xg.rearrange("p (c t) -> p c t", c=KD),
                             in_=x[:, t0:t0 + PW]
                                 .rearrange("(c p) t -> c p t", p=128)
                                 .rearrange("c p t -> p c t")),
                 waits=[ln_sig], incs=[(ldx, 16)])
            nldx += 16
            ln_sig = emit_ln_fm(p, c, xg, xsq, xln_ext, stats_ps, mb_ps, rb_ps,
                                rowbuf, scr, s_mm, mmc, PW,
                                pre_waits=[(ldx, nldx)],
                                xln_stride=EXT3, xln_off=t0)

        smfree = None
        for g4 in range(4):
            t0 = 3 + g4 * 512
            for kcx in range(KD):
                wts = []
                if kcx == 0:
                    wts = [ln_sig, (ld_w, WRDY0), smfree]
                p.op("tensor", lambda e, kcx=kcx, t0=t0: e.matmul(
                        sm_ps[0:96, 0:512],
                        wsm_s[:, kcx * 96:(kcx + 1) * 96],
                        xln_ext[:, kcx * EXT3 + t0:kcx * EXT3 + t0 + 512],
                        start=(kcx == 0), stop=(kcx == KD - 1)),
                     waits=wts, incs=([(s_mm, 1)] if kcx == KD - 1 else []))
            mmc["n"] += 1
            p.op("scalar", lambda e, g4=g4: e.activation(
                    bsg_sb[0:8, g4 * 512:(g4 + 1) * 512], sm_ps[0:8, 0:512],
                    AF.Sigmoid),
                 waits=[(s_mm, mmc["n"])])
            p.op("scalar", lambda e, g4=g4: e.copy(
                    g_sb[0:4, g4 * 512:(g4 + 1) * 512], sm_ps[32:36, 0:512]))
            smfree = p.op("scalar", lambda e, g4=g4: e.copy(
                    h_sb[0:4, g4 * 512:(g4 + 1) * 512], sm_ps[64:68, 0:512]))
        for (t, nm) in ((bsg_sb, bsg), (g_sb, gout), (h_sb, hout)):
            p.op("sync", lambda e, t=t, nm=nm: e.dma_start(out=nm, in_=t[:]),
                 waits=[smfree], incs=[(s_st, 16)])
            nst += 1

        pmm = 0
        ev_sigs = {}
        nw2 = 0
        praw_free = {}
        sl_last = {}
        for ti, (whbm, pout) in enumerate(zip(wqkv, pouts)):
            wguard = ev_sigs.get("wlast")
            for kcx in range(KD):
                p.op("gpsimd", lambda e, kcx=kcx, whbm=whbm: e.dma_start(
                        out=w_sb[:, kcx * D:(kcx + 1) * D],
                        in_=whbm[kcx * 128:(kcx + 1) * 128, :]),
                     waits=([wguard] if (kcx == 0 and wguard is not None) else []),
                     incs=[(ld_w2, 16)], dma=True)
                nw2 += 16
            wld = nw2
            for mc in range(KD):
                par = pmm % 2
                for g in range(NPG):
                    for kcx in range(KD):
                        wts = []
                        if kcx == 0:
                            wts = [(ld_w2, wld), ln_sig, ev_sigs.get(par)]
                        p.op("tensor", lambda e, par=par, mc=mc, g=g, kcx=kcx:
                             e.matmul(mm[par][:, 0:PW],
                                      w_sb.rearrange("p (kc f) -> p kc f", f=D)
                                          [:, kcx, mc * 128:(mc + 1) * 128],
                                      xln_ext[:, kcx * EXT3 + g * PW:
                                              kcx * EXT3 + (g + 1) * PW],
                                      start=(kcx == 0), stop=(kcx == KD - 1)),
                             waits=wts,
                             incs=([(s_mm, 1)] if kcx == KD - 1 else []))
                    mmc["n"] += 1
                    ppar = mc % 2
                    ev_sigs[par] = p.op("scalar",
                         lambda e, par=par, ppar=ppar, g=g:
                         e.activation(praw[:, ppar * EXT3 + g * PW:
                                           ppar * EXT3 + (g + 1) * PW],
                                      mm[par][:, 0:PW], AF.Copy),
                         waits=[(s_mm, mmc["n"])] +
                               ([praw_free.get(mc % 2)] if g == 0 else []))
                    pmm += 1
                    par = pmm % 2
                ev_sigs["wlast"] = ev_sigs[(pmm - 1) % 2]
                ceng = "vector"
                cacc = cacc_g if ceng == "gpsimd" else cacc_v
                ppar = mc % 2
                pr_sig = ev_sigs[(pmm - 1) % 2]
                p.op(ceng, lambda e, ppar=ppar, mc=mc, ti=ti, cacc=cacc:
                     e.tensor_scalar(
                        cacc[:], praw[:, ppar * EXT3 + 0:ppar * EXT3 + TOK],
                        conv_s[:, mc * 12 + ti * 4:mc * 12 + ti * 4 + 1],
                        None, ALU.mult),
                     waits=[pr_sig, sl_last.get(ceng)])
                csig = None
                for j in (1, 2, 3):
                    csig = p.op(ceng, lambda e, ppar=ppar, mc=mc, ti=ti, j=j,
                                cacc=cacc:
                         e.scalar_tensor_tensor(
                             cacc[:], praw[:, ppar * EXT3 + j:ppar * EXT3 + j + TOK],
                             conv_s[:, mc * 12 + ti * 4 + j:
                                    mc * 12 + ti * 4 + j + 1],
                             cacc[:], ALU.mult, ALU.add))
                stguard = (s_st, 16 * (3 + ti * 8)) if ti > 0 else None
                sl_sig = p.op("scalar", lambda e, pout=pout, mc=mc, cacc=cacc:
                     e.activation(
                        pout[:, mc * TOK:(mc + 1) * TOK], cacc[:], AF.Silu),
                     waits=[csig, stguard])
                sl_last[ceng] = sl_sig
                praw_free[mc % 2] = csig
                p.op("sync", lambda e, pout=pout, mc=mc, ti=ti: e.dma_start(
                        out=outs[ti][mc * 128:(mc + 1) * 128, :],
                        in_=pout[:, mc * TOK:(mc + 1) * TOK]),
                     waits=[sl_sig], incs=[(s_st, 16)])
                nst += 1
        p.op("sync", lambda e: e.nop(), waits=[(s_st, nst * 16)])
        p.emit(c.s_const, c.ready_val)
    return nc


def run_l1front(x2_full, Wq, Wk, Wv, conv_q, conv_k, conv_v, Wb, Wg, Wh):
    nc = build_l1front()
    f32 = lambda a: np.asarray(a, np.float32)
    x2 = f32(x2_full)
    xin = np.zeros((NCORES, EXT3, D), np.float32)
    for i in range(NCORES):
        j, s = i // 2, i % 2
        xin[i, 3:3 + TOK] = x2[j, s * TOK:(s + 1) * TOK]
        if s == 1:
            xin[i, 0:3] = x2[j, TOK - 3:TOK]
    xin_t = np.ascontiguousarray(xin.transpose(0, 2, 1))
    cv = np.zeros((128, 8 * 12), np.float32)
    for kcx in range(8):
        for ti, cw in enumerate((conv_q, conv_k, conv_v)):
            cv[:, kcx * 12 + ti * 4:kcx * 12 + (ti + 1) * 4] = \
                f32(cw)[kcx * 128:(kcx + 1) * 128, :]
    wsm_np = np.zeros((D, 96), np.float32)
    wsm_np[:, 0:8] = f32(Wb)
    wsm_np[:, 32:36] = f32(Wg)
    wsm_np[:, 64:68] = f32(Wh)
    in_maps = [{"x": xin_t[i], "wq": f32(Wq), "wk": f32(Wk), "wv": f32(Wv),
                "convs": cv, "wsm": wsm_np} for i in range(NCORES)]
    res = run_bass_kernel_spmd(nc, in_maps, core_ids=list(range(NCORES)))
    r = res.results
    cat = lambda key: np.stack([rr[key] for rr in r])
    return cat("qc"), cat("kc"), cat("vc"), cat("bsg"), cat("gout"), cat("hout")


def build_l1comb():
    nc = bass.Bass("TRN2", target_bir_lowering=False, debug=False)
    o_in = nc.dram_tensor("o_in", [TOK, D], F32, kind="ExternalInput").ap()
    v_in = nc.dram_tensor("v_in", [TOK, D], BF16, kind="ExternalInput").ap()
    gh_in = nc.dram_tensor("gh_in", [TOK, 8], F32, kind="ExternalInput").ap()
    bgh = nc.dram_tensor("bgh", [128, 8], F32, kind="ExternalInput").ap()
    nw_in = nc.dram_tensor("nw_in", [128, 1024], F32, kind="ExternalInput").ap()
    wo = nc.dram_tensor("wo", [D, D], F32, kind="ExternalInput").ap()
    x = nc.dram_tensor("x", [D, TOK], F32, kind="ExternalInput").ap()
    y = nc.dram_tensor("y", [D, TOK], F32, kind="ExternalOutput").ap()
    c = Consts(nc)

    with ExitStack() as st:
        sb = lambda name, shp, dt: st.enter_context(nc.sbuf_tensor(name, shp, dt))
        ps = lambda name, shp, dt: st.enter_context(nc.psum_tensor(name, shp, dt))
        wo_s = sb("wo_s", [128, KD * D], BF16)
        nw_s = sb("nw_s", [128, 1024], F32)
        bgh_s = sb("bgh_s", [128, 8], F32)
        ot = sb("ot", [128, 1024], F32)
        vt = sb("vt", [128, 1024], BF16)
        ght = sb("ght", [128, 8], F32)
        wrk = sb("wrk", [128, 24], F32)
        o2 = sb("o2", [128, 1024], F32)
        ssq = sb("ssq", [128, 8], F32)
        o3 = sb("o3", [128, 1024], BF16)
        ofm = sb("ofm", [128, KD * TOK], BF16)
        yt = sb("yt", [128, 2 * 512], F32)
        xr = sb("xr", [128, 2 * 512], F32)
        mm = [ps("mmA", [128, 512], F32), ps("mmB", [128, 512], F32)]
        pq = ps("pqA", [128, 1024], BF16)

        p = Prog(nc)
        c.materialize(p, st)
        ld_w = p.sem("ld_w")
        ld_wh = p.sem("ld_wh")
        ld_t = p.sem("ld_t")
        ldr = p.sem("ldr")
        s_tq = p.sem("s_tq")
        s_mm = p.sem("s_mm")
        s_st = [p.sem("s_st0"), p.sem("s_st1")]
        nstp = [0, 0]
        mmc = {"n": 0}
        tqc = {"n": 0}

        nw = 0
        for kcx in range(KD):
            p.op("gpsimd", lambda e, kcx=kcx: e.dma_start(
                    out=wo_s[:, kcx * D:(kcx + 1) * D],
                    in_=wo[kcx * 128:(kcx + 1) * 128, :]),
                 incs=[(ld_w, 16)], dma=True)
            nw += 16
        p.op("sync", lambda e: e.dma_start(out=nw_s[:], in_=nw_in),
             incs=[(ld_wh, 16)])
        p.op("sync", lambda e: e.dma_start(out=bgh_s[:], in_=bgh),
             incs=[(ld_wh, 16)])
        WRDY = nw

        nldt = 0
        ev_sig = None
        tile_sigs = {}
        for i in range(16):
            tguard = tile_sigs.get(i - 1)
            p.op("sync", lambda e, i=i: e.dma_start(
                    out=ot[:], in_=o_in[i * 128:(i + 1) * 128, :]),
                 waits=[tguard], incs=[(ld_t, 16)])
            nldt += 16
            p.op("sync", lambda e, i=i: e.dma_start(
                    out=vt[:], in_=v_in[i * 128:(i + 1) * 128, :]),
                 incs=[(ld_t, 16)])
            nldt += 16
            p.op("sync", lambda e, i=i: e.dma_start(
                    out=ght[:], in_=gh_in[i * 128:(i + 1) * 128, :]),
                 incs=[(ld_t, 16)])
            nldt += 16
            ldw = [(ld_t, nldt), (ld_w, WRDY), (ld_wh, 32)]
            p.op("vector", lambda e: e.tensor_add(wrk[:, 0:4], ght[:, 0:4],
                                                  bgh_s[:, 0:4]), waits=ldw)
            p.op("vector", lambda e: e.tensor_add(wrk[:, 4:8], ght[:, 4:8],
                                                  bgh_s[:, 4:8]))
            p.op("vector", lambda e: e.tensor_max(wrk[:, 8:12], wrk[:, 0:4],
                                                  wrk[:, 4:8]))
            p.op("vector", lambda e: e.tensor_scalar(wrk[:, 8:12], wrk[:, 8:12],
                                                     0.0, None, ALU.max))
            p.op("vector", lambda e: e.tensor_sub(wrk[:, 0:4], wrk[:, 0:4],
                                                  wrk[:, 8:12]))
            vsb = p.op("vector", lambda e: e.tensor_sub(wrk[:, 4:8], wrk[:, 4:8],
                                                        wrk[:, 8:12]))
            p.op("scalar", lambda e: e.activation(wrk[:, 12:16], wrk[:, 0:4],
                                                  AF.Exp), waits=[vsb])
            p.op("scalar", lambda e: e.activation(wrk[:, 16:20], wrk[:, 4:8],
                                                  AF.Exp))
            es = p.op("scalar", lambda e: e.activation(wrk[:, 20:24], wrk[:, 8:12],
                                                       AF.Exp, scale=-1.0))
            p.op("vector", lambda e: e.tensor_add(wrk[:, 12:16], wrk[:, 12:16],
                                                  wrk[:, 16:20]), waits=[es])
            p.op("vector", lambda e: e.tensor_add(wrk[:, 12:16], wrk[:, 12:16],
                                                  wrk[:, 20:24]))
            rcp = p.op("vector", lambda e: e.reciprocal(wrk[:, 16:20],
                                                        wrk[:, 12:16]))
            wr = p.op("vector", lambda e: e.tensor_mul(wrk[:, 20:24],
                                                       wrk[:, 20:24],
                                                       wrk[:, 16:20]),
                      waits=[rcp])
            o2s = None
            for h in range(4):
                o2s = p.op("vector", lambda e, h=h: e.scalar_tensor_tensor(
                        o2[:, h * 256:(h + 1) * 256],
                        vt[:, h * 256:(h + 1) * 256],
                        wrk[:, 20 + h:21 + h],
                        ot[:, h * 256:(h + 1) * 256], ALU.mult, ALU.add),
                     waits=([wr] if h == 0 else ()))
            sqs = p.op("vector", lambda e: e.tensor_mul(yt[:, 0:1024],
                                                        o2[:, 0:1024],
                                                        o2[:, 0:1024]),
                       waits=[o2s])
            p.op("vector", lambda e: e.tensor_reduce(
                    ssq[:, 0:4], yt[:, 0:1024].rearrange("p (a t) -> p a t", a=4),
                    AX.X, ALU.add))
            tss = p.op("vector", lambda e: e.tensor_scalar(
                    ssq[:, 0:4], ssq[:, 0:4], 1.0 / 256.0,
                    EPS, ALU.mult, ALU.add))
            lns = p.op("scalar", lambda e: e.activation(ssq[:, 4:8], ssq[:, 0:4],
                                                        AF.Ln), waits=[tss])
            rst = p.op("scalar", lambda e: e.activation(ssq[:, 4:8], ssq[:, 4:8],
                                                        AF.Exp, scale=-0.5),
                       waits=[lns])
            for h in range(4):
                p.op("vector", lambda e, h=h: e.tensor_scalar(
                        o2[:, h * 256:(h + 1) * 256], o2[:, h * 256:(h + 1) * 256],
                        ssq[:, 4 + h:5 + h], None, ALU.mult),
                     waits=([rst] if h == 0 else ()))
            o3s = p.op("vector", lambda e: e.tensor_mul(o3[:], o2[:, 0:1024],
                                                        nw_s[:]))
            for quad in range(2):
                for jj in range(4):
                    cc = quad * 4 + jj
                    wts = []
                    if jj == 0:
                        wts = [o3s, ev_sig]
                    p.op("tensor", lambda e, cc=cc, jj=jj: e.transpose(
                            pq[:, jj * 128:(jj + 1) * 128],
                            o3[:, cc * 128:(cc + 1) * 128], c.ident_bf),
                         waits=wts, incs=([(s_tq, 1)] if jj == 3 else []))
                tqc["n"] += 1
                ev_sig = p.op("vector", lambda e, quad=quad, i=i: e.tensor_copy(
                        ofm.rearrange("p (cc t) -> p cc t", cc=KD)
                           [:, quad * 4:(quad + 1) * 4, i * 128:(i + 1) * 128],
                        pq[:, 0:512].rearrange("p (a f) -> p a f", a=4)),
                     waits=[(s_tq, tqc["n"])])
            tile_sigs[i] = ev_sig

        EW = 512
        nst = 0
        nldr = 0
        est = {}
        pmm = 0
        for g in range(4):
            for mc in range(KD):
                par = pmm % 2
                p.op("sync", lambda e, mc=mc, g=g, par=par:
                     e.dma_start(out=xr[:, par * EW:(par + 1) * EW],
                                 in_=x[mc * 128:(mc + 1) * 128,
                                       g * EW:(g + 1) * EW]),
                     waits=[est.get(par)], incs=[(ldr, 16)])
                nldr += 16
                for kcx in range(KD):
                    wts = []
                    if kcx == 0:
                        wts = [ev_sig, est.get(par), (ld_w, WRDY), (ld_wh, 32)]
                    p.op("tensor", lambda e, par=par, mc=mc, g=g, kcx=kcx:
                         e.matmul(mm[par][:, 0:EW],
                                  wo_s.rearrange("p (kc f) -> p kc f", f=D)
                                      [:, kcx, mc * 128:(mc + 1) * 128],
                                  ofm[:, kcx * TOK + g * EW:
                                      kcx * TOK + (g + 1) * EW],
                                  start=(kcx == 0), stop=(kcx == KD - 1)),
                         waits=wts,
                         incs=([(s_mm, 1)] if kcx == KD - 1 else []))
                mmc["n"] += 1
                epar = nst % 2
                esig = p.op("vector", lambda e, par=par, epar=epar: e.tensor_add(
                        yt[:, epar * EW:(epar + 1) * EW], mm[par][:, 0:EW],
                        xr[:, par * EW:(par + 1) * EW]),
                     waits=[(s_mm, mmc["n"]), (ldr, nldr),
                            (s_st[epar], 16 * nstp[epar])])
                est[par] = esig
                pmm += 1
                p.op("sync", lambda e, epar=epar, mc=mc, g=g: e.dma_start(
                        out=y[mc * 128:(mc + 1) * 128, g * EW:(g + 1) * EW],
                        in_=yt[:, epar * EW:(epar + 1) * EW]),
                     waits=[esig], incs=[(s_st[epar], 16)])
                nstp[epar] += 1
                nst += 1
        p.op("sync", lambda e: e.nop(),
             waits=[(s_st[0], nstp[0] * 16), (s_st[1], nstp[1] * 16)])
        p.emit(c.s_const, c.ready_val)
    return nc


def run_l1comb(o_full, v_full, g_full, hg_full, bg, bh, norm_w, Wo, x2_full):
    import ml_dtypes
    nc = build_l1comb()
    f32 = lambda a: np.asarray(a, np.float32)
    x2 = f32(x2_full)
    in_maps = []
    for i in range(NCORES):
        j, s = i // 2, i % 2
        sl = slice(s * TOK, (s + 1) * TOK)
        gh = np.concatenate([f32(g_full)[j, sl], f32(hg_full)[j, sl]],
                            axis=1)
        bghv = np.zeros((128, 8), np.float32)
        bghv[:, 0:4] = f32(bg)[None, :]
        bghv[:, 4:8] = f32(bh)[None, :]
        nwv = np.broadcast_to(f32(norm_w)[None, :], (128, 256))
        nwt = np.ascontiguousarray(np.tile(nwv, (1, 4)))
        in_maps.append({
            "o_in": np.ascontiguousarray(f32(o_full)[j, sl]),
            "v_in": np.ascontiguousarray(
                np.asarray(v_full, ml_dtypes.bfloat16)[j, sl]),
            "gh_in": np.ascontiguousarray(gh),
            "bgh": bghv,
            "nw_in": nwt,
            "wo": f32(Wo),
            "x": np.ascontiguousarray(x2[j, sl].T),
        })
    res = run_bass_kernel_spmd(nc, in_maps, core_ids=list(range(NCORES)))
    yv = np.stack([r["y"] for r in res.results])
    return np.ascontiguousarray(yv.transpose(0, 2, 1)).reshape(4, 4096, D)


def build_l1scan():
    nc = bass.Bass("TRN2", target_bir_lowering=False, debug=False)
    L4L = 4096
    NG = 32
    qtm_d = [nc.dram_tensor(f"qtm{i}", [128, NG * 256], BF16,
                            kind="ExternalInput").ap() for i in range(2)]
    ktm_d = [nc.dram_tensor(f"ktm{i}", [128, NG * 256], BF16,
                            kind="ExternalInput").ap() for i in range(2)]
    vtm_d = [nc.dram_tensor(f"vtm{i}", [128, NG * 256], BF16,
                            kind="ExternalInput").ap() for i in range(2)]
    bl_d = [nc.dram_tensor(f"bl{i}", [128, NG], F32,
                           kind="ExternalInput").ap() for i in range(2)]
    bg_d = [nc.dram_tensor(f"bg{i}", [128, NG], F32,
                           kind="ExternalInput").ap() for i in range(2)]
    gg_d = [nc.dram_tensor(f"gg{i}", [128, NG], F32,
                           kind="ExternalInput").ap() for i in range(2)]
    hh_d = [nc.dram_tensor(f"hh{i}", [128, NG], F32,
                           kind="ExternalInput").ap() for i in range(2)]
    bgh_d = nc.dram_tensor("bgh", [128, 4], F32, kind="ExternalInput").ap()
    msk_d = nc.dram_tensor("msk", [128, 416], F32, kind="ExternalInput").ap()
    o_out = nc.dram_tensor("o_out", [2 * NG * 128, 256], F32,
                           kind="ExternalOutput").ap()
    scr1 = nc.dram_tensor("scr1", [128, 4096], F32, kind="Internal").ap()
    scr2 = nc.dram_tensor("scr2", [128, 1024], BF16, kind="Internal").ap()
    c = Consts(nc)

    with ExitStack() as st:
        sb = lambda name, shp, dt: st.enter_context(nc.sbuf_tensor(name, shp, dt))
        qtm = sb("qtm", [128, NG * 256], BF16)
        ktm = sb("ktm", [128, NG * 256], BF16)
        vtm = sb("vtm", [128, NG * 256], BF16)
        q_fm = sb("q_fm", [128, 2 * L4L], BF16)
        k_fm = sb("k_fm", [128, 2 * L4L], BF16)
        TB = [sb("TB_l", [128, NG * 128], BF16), sb("TB_g", [128, NG * 128], BF16)]
        T0F = [sb("T0F_l", [128, 4096], F32), sb("T0F_g", [128, 4096], F32)]
        T_vec = sb("T_vec", [128, 1024], F32)
        T_vt = sb("T_vt", [128, 1024], F32)
        T_bfs = sb("T_bfs", [128, 1024], BF16)
        kkm_sb = sb("kkm_sb", [128, 128], F32)
        acc32 = sb("acc32", [128, 32], F32)
        blg = [sb("bl_s", [128, NG], F32), sb("bg_s", [128, NG], F32)]
        nblg = [sb("nbl_s", [128, NG], F32), sb("nbg_s", [128, NG], F32)]
        ggs = sb("ggs", [128, NG], F32)
        hhs = sb("hhs", [128, NG], F32)
        wlg = [sb("wl_s", [128, NG], F32), sb("wg_s", [128, NG], F32)]
        gwk = sb("gwk", [128, 8 * NG], F32)
        bghs = sb("bghs", [128, 4], F32)
        msk = sb("msk_s", [128, 416], F32)
        nrm = sb("nrm", [128, 2 * NG], F32)
        vs_sb = sb("vs_sb", [128, 512], BF16)
        kb_sb = sb("kb_sb", [128, 512], BF16)
        wm_sb = sb("wm_sb", [128, 512], BF16)
        wkm_sb = [sb("wkm_l", [128, 128], BF16), sb("wkm_g", [128, 128], BF16)]
        wkm3 = [sb("wkm3_l", [128, 128], BF16), sb("wkm3_g", [128, 128], BF16)]
        grm_sb = sb("grm_sb", [128, 128], BF16)
        Usb = [sb("U_l", [128, 256], BF16), sb("U_g", [128, 256], BF16)]
        S_bf = [sb("Sbf_l", [128, 512], BF16), sb("Sbf_g", [128, 512], BF16)]
        S_sb = [sb("Ssb_l", [128, 512], F32), sb("Ssb_g", [128, 512], F32)]
        o_stage = sb("o_stage", [128, 2 * 256], F32)

        p = Prog(nc)
        c.materialize(p, st)
        ld_in = p.sem("ld_in")
        ld_sh = p.sem("ld_sh")
        s_tq = p.sem("s_tq")
        s_mm = p.sem("s_mm")
        s_st = [p.sem("s_st0"), p.sem("s_st1")]
        nstp = [0, 0]
        mmc = {"n": 0}
        tqc = {"n": 0}
        nld = {"n": 0}
        nsh = {"n": 0}

        MS_BD = msk[:, 32:160]
        MS_BL = msk[:, 160:288]
        MS_CA = msk[:, 288:416]
        p.op("sync", lambda e: e.dma_start(out=msk[:], in_=msk_d),
             incs=[(ld_in, 16)])
        nld["n"] += 16
        p.op("sync", lambda e: e.dma_start(out=bghs[:], in_=bgh_d),
             incs=[(ld_in, 16)])
        nld["n"] += 16
        mz_sig = None
        for t in (Usb[0], Usb[1], S_bf[0], S_bf[1], wkm3[0], wkm3[1],
                  TB[0], TB[1]):
            mz_sig = p.op("vector", lambda e, t=t: e.memset(t[:], 0.0))

        head_free = None
        shuf_done = None
        for hh in range(2):
            for (dst, src) in ((qtm, qtm_d[hh]), (ktm, ktm_d[hh]),
                               (vtm, vtm_d[hh])):
                p.op("sync", lambda e, dst=dst, src=src:
                     e.dma_start(out=dst[:], in_=src),
                     waits=[head_free], incs=[(ld_in, 16)])
                nld["n"] += 16
            for (dst, src) in ((blg[0], bl_d[hh]), (blg[1], bg_d[hh]),
                               (ggs, gg_d[hh]), (hhs, hh_d[hh])):
                p.op("sync", lambda e, dst=dst, src=src:
                     e.dma_start(out=dst[:], in_=src),
                     incs=[(ld_in, 16)])
                nld["n"] += 16
            LDV = nld["n"]

            for (ti, tsr) in ((0, qtm), (1, ktm)):
                for g in range(NG):
                    p.op("vector", lambda e, tsr=tsr, g=g: e.tensor_mul(
                            o_stage[:, 0:256], tsr[:, 256 * g:256 * (g + 1)],
                            tsr[:, 256 * g:256 * (g + 1)]),
                         waits=([(ld_in, LDV), mz_sig,
                                 (s_st[0], 16 * nstp[0]),
                                 (s_st[1], 16 * nstp[1])]
                                if (ti == 0 and g == 0) else []))
                    p.op("vector", lambda e, ti=ti, g=g: e.tensor_reduce(
                            nrm[:, ti * NG + g:ti * NG + g + 1],
                            o_stage[:, 0:256], AX.X, ALU.add))
            ns = p.op("vector", lambda e: e.tensor_scalar(
                    nrm[:], nrm[:], 1e-12, None, ALU.add))
            p.op("scalar", lambda e: e.activation(nrm[:], nrm[:], AF.Ln),
                 waits=[ns])
            rs = p.op("scalar", lambda e: e.activation(nrm[:], nrm[:], AF.Exp,
                                                       scale=-0.5))
            sc_sig = None
            for (ti, tsr) in ((0, qtm), (1, ktm)):
                for g in range(NG):
                    sc_sig = p.op("vector", lambda e, ti=ti, tsr=tsr, g=g:
                         e.tensor_scalar(
                            tsr[:, 256 * g:256 * (g + 1)],
                            tsr[:, 256 * g:256 * (g + 1)],
                            nrm[:, ti * NG + g:ti * NG + g + 1], None, ALU.mult),
                         waits=([rs] if (ti == 0 and g == 0) else ()))
            p.op("vector", lambda e: e.tensor_scalar(nblg[0][:], blg[0][:],
                                                     -1.0, None, ALU.mult))
            nb_sig = p.op("vector", lambda e: e.tensor_scalar(
                    nblg[1][:], blg[1][:], -1.0, None, ALU.mult))
            W0 = gwk
            p.op("vector", lambda e, hh=hh: e.tensor_scalar(
                    W0[:, 0:NG], ggs[:], bghs[:, 2 * hh:2 * hh + 1], None,
                    ALU.add))
            p.op("vector", lambda e, hh=hh: e.tensor_scalar(
                    W0[:, NG:2 * NG], hhs[:], bghs[:, 2 * hh + 1:2 * hh + 2],
                    None, ALU.add))
            p.op("vector", lambda e: e.tensor_max(W0[:, 2 * NG:3 * NG],
                                                  W0[:, 0:NG],
                                                  W0[:, NG:2 * NG]))
            p.op("vector", lambda e: e.tensor_scalar(
                    W0[:, 2 * NG:3 * NG], W0[:, 2 * NG:3 * NG], 0.0, None,
                    ALU.max))
            p.op("vector", lambda e: e.tensor_sub(W0[:, 0:NG], W0[:, 0:NG],
                                                  W0[:, 2 * NG:3 * NG]))
            sb_sig = p.op("vector", lambda e: e.tensor_sub(
                    W0[:, NG:2 * NG], W0[:, NG:2 * NG], W0[:, 2 * NG:3 * NG]))
            p.op("scalar", lambda e: e.activation(W0[:, 3 * NG:4 * NG],
                                                  W0[:, 0:NG], AF.Exp),
                 waits=[sb_sig])
            p.op("scalar", lambda e: e.activation(W0[:, 4 * NG:5 * NG],
                                                  W0[:, NG:2 * NG], AF.Exp))
            ex_sig = p.op("scalar", lambda e: e.activation(
                    W0[:, 5 * NG:6 * NG], W0[:, 2 * NG:3 * NG], AF.Exp,
                    scale=-1.0))
            p.op("vector", lambda e: e.tensor_add(W0[:, 6 * NG:7 * NG],
                                                  W0[:, 3 * NG:4 * NG],
                                                  W0[:, 4 * NG:5 * NG]),
                 waits=[ex_sig])
            p.op("vector", lambda e: e.tensor_add(W0[:, 6 * NG:7 * NG],
                                                  W0[:, 6 * NG:7 * NG],
                                                  W0[:, 5 * NG:6 * NG]))
            p.op("vector", lambda e: e.reciprocal(W0[:, 7 * NG:8 * NG],
                                                  W0[:, 6 * NG:7 * NG]))
            p.op("vector", lambda e: e.tensor_mul(wlg[0][:],
                                                  W0[:, 3 * NG:4 * NG],
                                                  W0[:, 7 * NG:8 * NG]))
            gate_sig = p.op("vector", lambda e: e.tensor_mul(
                    wlg[1][:], W0[:, 4 * NG:5 * NG], W0[:, 7 * NG:8 * NG]))

            with nc.psum_tensor(f"pq{hh}", [128, 1024], BF16) as pq, \
                 nc.psum_tensor(f"kk_ps{hh}", [128, 512], F32) as kk_ps:
                ev_sig = None
                for (tsr, fmt) in ((qtm, q_fm), (ktm, k_fm)):
                    for g in range(NG):
                        par = g % 2
                        for cc in range(2):
                            wts = []
                            if cc == 0:
                                wts = [sc_sig, ev_sig]
                            p.op("tensor", lambda e, tsr=tsr, g=g, cc=cc,
                                 par=par:
                                 e.transpose(pq[:, par * 256 + cc * 128:
                                                par * 256 + (cc + 1) * 128],
                                             tsr[:, 256 * g + 128 * cc:
                                                 256 * g + 128 * (cc + 1)],
                                             c.ident_bf),
                                 waits=wts,
                                 incs=([(s_tq, 1)] if cc == 1 else []))
                        tqc["n"] += 1
                        for cc in range(2):
                            ev_sig = p.op("vector",
                                 lambda e, fmt=fmt, g=g, cc=cc, par=par:
                                 e.tensor_copy(
                                     fmt[:, cc * L4L + 128 * g:
                                         cc * L4L + 128 * (g + 1)],
                                     pq[:, par * 256 + cc * 128:
                                        par * 256 + (cc + 1) * 128]),
                                 waits=[(s_tq, tqc["n"])])
                t0_sig = None
                for g in range(NG):
                    for cc in range(2):
                        wts = []
                        if cc == 0:
                            wts = [ev_sig, nb_sig, t0_sig]
                        p.op("tensor", lambda e, g=g, cc=cc: e.matmul(
                                kk_ps[:, 0:128],
                                k_fm[:, cc * L4L + 128 * g:
                                     cc * L4L + 128 * (g + 1)],
                                k_fm[:, cc * L4L + 128 * g:
                                     cc * L4L + 128 * (g + 1)],
                                start=(cc == 0), stop=(cc == 1)),
                             waits=wts,
                             incs=([(s_mm, 1)] if cc == 1 else []))
                    mmc["n"] += 1
                    p.op("vector", lambda e: e.tensor_mul(
                            kkm_sb[:], kk_ps[:, 0:128], MS_BD),
                         waits=[(s_mm, mmc["n"])])
                    for s in range(2):
                        t0_sig = p.op("vector", lambda e, g=g, s=s:
                             e.tensor_scalar(
                                T0F[s][:, 128 * g:128 * (g + 1)], kkm_sb[:],
                                nblg[s][:, g:g + 1], None, ALU.mult))
                prev_tv_done = None
                for s in range(2):
                    p.op("sync", lambda e, s=s: e.dma_start(out=scr1,
                                                            in_=T0F[s][:]),
                         waits=[t0_sig, shuf_done], incs=[(ld_sh, 16)])
                    nsh["n"] += 16
                    WSC = nsh["n"]
                    for a in range(4):
                        p.op("sync", lambda e, a=a: e.dma_start(
                                out=T_vec[32 * a:32 * (a + 1), :]
                                    .rearrange("p (r cc) -> p r cc", r=32),
                                in_=scr1.rearrange(
                                    "(A r) (g blk cc) -> A blk g r cc",
                                    A=4, blk=4, cc=32)[a, a]),
                             waits=([prev_tv_done, (ld_sh, WSC)] if a == 0
                                    else [(ld_sh, WSC)]),
                             incs=[(ld_sh, 16)])
                        nsh["n"] += 16
                    rec_sig = None
                    first_rec = [(ld_sh, nsh["n"])]
                    for i in range(1, 32):
                        for k2 in range(i):
                            if i == 1:
                                src_out = T_vec[:, 32:64]
                                in1 = T_vec[:, 32:64]
                            elif k2 == 0:
                                src_out = acc32[:]
                                in1 = T_vec[:, 32 * i:32 * (i + 1)]
                            elif k2 == i - 1:
                                src_out = T_vec[:, 32 * i:32 * (i + 1)]
                                in1 = acc32[:]
                            else:
                                src_out = acc32[:]
                                in1 = acc32[:]
                            rec_sig = p.op("vector",
                                 lambda e, i=i, k2=k2, src_out=src_out, in1=in1:
                                 e.scalar_tensor_tensor(
                                     src_out, T_vec[:, 32 * k2:32 * (k2 + 1)],
                                     T_vec[:, 32 * i + k2:32 * i + k2 + 1],
                                     in1, ALU.mult, ALU.add),
                                 waits=(first_rec if (i == 1 and k2 == 0)
                                        else ()))
                            first_rec = []
                    p.op("vector", lambda e: e.tensor_scalar(
                            T_vec[:, 0:1024:33], T_vec[:, 0:1024:33], 1.0,
                            None, ALU.add))
                    prev_tv_done = p.op("vector", lambda e: e.tensor_copy(
                            T_vt[:].rearrange("p (cc r) -> p cc r", cc=32),
                            T_vec[:].rearrange("p (r cc) -> p r cc", r=32)
                                    .rearrange("p r cc -> p cc r")))
                    tb_sig = p.op("vector", lambda e: e.tensor_copy(
                            T_bfs[:], T_vt[:]))
                    p.op("sync", lambda e: e.dma_start(out=scr2, in_=T_bfs[:]),
                         waits=[tb_sig], incs=[(ld_sh, 16)])
                    nsh["n"] += 16
                    WSC2 = nsh["n"]
                    for a in range(4):
                        p.op("sync", lambda e, s=s, a=a: e.dma_start(
                                out=TB[s][32 * a:32 * (a + 1), :]
                                    .rearrange("k (g blk r) -> k g blk r",
                                               blk=4, r=32)[:, :, a, :],
                                in_=scr2.rearrange("(A g) (k r) -> A g k r",
                                                   A=4, r=32)[a]
                                        .rearrange("g k r -> k g r")),
                             waits=[(ld_sh, WSC2), mz_sig],
                             incs=[(ld_sh, 16)])
                        nsh["n"] += 16
                    shuf_done = (ld_sh, nsh["n"])
            SHUF = (ld_sh, nsh["n"])

            p.op("vector", lambda e: e.memset(S_bf[0][:], 0.0),
                 waits=[head_free])
            p.op("vector", lambda e: e.memset(S_bf[1][:], 0.0))
            p.op("vector", lambda e: e.memset(S_sb[0][:], 0.0))
            sbz = p.op("vector", lambda e: e.memset(S_sb[1][:], 0.0))
            with nc.psum_tensor(f"sd_l{hh}", [128, 512], F32) as sd_l, \
                 nc.psum_tensor(f"sd_g{hh}", [128, 512], F32) as sd_g, \
                 nc.psum_tensor(f"u_l{hh}", [128, 512], F32) as u_l, \
                 nc.psum_tensor(f"u_g{hh}", [128, 512], F32) as u_g, \
                 nc.psum_tensor(f"o_ps{hh}", [128, 512], F32) as o_ps, \
                 nc.psum_tensor(f"gram{hh}", [128, 512], F32) as gram:
                sd_ps = [sd_l, sd_g]
                u_ps = [u_l, u_g]
                vk_free = None
                wm_free = None
                gram_free = None
                sbf_sig = {0: sbz, 1: sbz}
                ext_sigs = {}
                o_ev = {0: None, 1: None}
                for g in range(NG):
                    p.op("vector", lambda e, g=g: e.tensor_scalar(
                            vs_sb[:, 0:256], vtm[:, 256 * g:256 * (g + 1)],
                            blg[0][:, g:g + 1], None, ALU.mult),
                         waits=[vk_free, gate_sig, SHUF])
                    p.op("vector", lambda e, g=g: e.tensor_scalar(
                            vs_sb[:, 256:512], vtm[:, 256 * g:256 * (g + 1)],
                            blg[1][:, g:g + 1], None, ALU.mult))
                    p.op("vector", lambda e, g=g: e.tensor_scalar(
                            kb_sb[:, 0:256], ktm[:, 256 * g:256 * (g + 1)],
                            nblg[0][:, g:g + 1], None, ALU.mult))
                    vk_sig = p.op("vector", lambda e, g=g: e.tensor_scalar(
                            kb_sb[:, 256:512], ktm[:, 256 * g:256 * (g + 1)],
                            nblg[1][:, g:g + 1], None, ALU.mult))
                    for s in range(2):
                        p.op("tensor", lambda e, s=s, g=g: e.matmul(
                                u_ps[s][:, 0:256], TB[s][:, 128 * g:128 * (g + 1)],
                                vs_sb[:, s * 256:(s + 1) * 256],
                                start=True, stop=False),
                             waits=([vk_sig, SHUF, sbf_sig[0], sbf_sig[1]]
                                    if s == 0 else []))
                    for s in range(2):
                        for cc in range(2):
                            p.op("tensor", lambda e, s=s, cc=cc, g=g: e.matmul(
                                    gram[:, s * 256 + cc * 128:
                                         s * 256 + (cc + 1) * 128],
                                    kb_sb[:, s * 256 + cc * 128:
                                          s * 256 + (cc + 1) * 128],
                                    TB[s][:, 128 * g:128 * (g + 1)],
                                    start=(s == 0 and cc == 0),
                                    stop=(s == 1 and cc == 1)),
                                 waits=([gram_free] if (s == 0 and cc == 0)
                                        else []),
                                 incs=([(s_mm, 1)] if (s == 1 and cc == 1)
                                       else []))
                    mmc["n"] += 1
                    wm_sig = p.op("vector", lambda e: e.tensor_copy(
                            wm_sb[:], gram[:, 0:512]),
                         waits=[(s_mm, mmc["n"])])
                    for s in range(2):
                        for cc in range(2):
                            p.op("tensor", lambda e, s=s, cc=cc: e.matmul(
                                    u_ps[s][:, 0:256],
                                    wm_sb[:, s * 256 + cc * 128:
                                          s * 256 + (cc + 1) * 128],
                                    S_bf[s][:, cc * 256:(cc + 1) * 256],
                                    start=False, stop=(cc == 1)),
                                 waits=([wm_sig] if (s == 0 and cc == 0)
                                        else []))
                    for s in range(2):
                        for cc in range(2):
                            p.op("tensor", lambda e, s=s, cc=cc, g=g: e.matmul(
                                    gram[:, s * 128:(s + 1) * 128],
                                    k_fm[:, cc * L4L + 128 * g:
                                         cc * L4L + 128 * (g + 1)],
                                    wm_sb[:, s * 256 + cc * 128:
                                          s * 256 + (cc + 1) * 128],
                                    start=(s == 0 and cc == 0),
                                    stop=(s == 1 and cc == 1)),
                                 waits=([wm_sig] if (s == 0 and cc == 0)
                                        else []),
                                 incs=([(s_mm, 1)] if (s == 1 and cc == 1)
                                       else []))
                    mmc["n"] += 1
                    wk3 = None
                    for s in range(2):
                        p.op("vector", lambda e, s=s: e.tensor_mul(
                                wkm_sb[s][:], gram[:, s * 128:(s + 1) * 128],
                                MS_BL),
                             waits=([(s_mm, mmc["n"])] if s == 0 else ()))
                        wk3 = p.op("vector", lambda e, s=s: e.tensor_copy(
                                wkm3[s][:, 96:128], wkm_sb[s][:, 96:128]))
                    wm_free = wk3
                    EXT_SLICES = [(0, 32), (32, 64), (64, 96), (64, 128)]
                    for s in range(2):
                        ext_sigs[(0, s)] = p.op("vector",
                             lambda e, s=s, g=g: e.tensor_scalar(
                                Usb[s][0:32, :], u_ps[s][0:32, 0:256],
                                wlg[s][0:32, g:g + 1], None, ALU.mult),
                             waits=[wk3] if s == 0 else [])
                    for a in range(1, 4):
                        for s in range(2):
                            if a < 3:
                                p.op("tensor", lambda e, s=s, a=a: e.matmul(
                                        u_ps[s][32 * a:32 * (a + 1), 0:256],
                                        wkm_sb[s][:, 32 * a:32 * (a + 1)],
                                        Usb[s][:],
                                        start=False, stop=True,
                                        skip_group_check=True),
                                     waits=[ext_sigs[(a - 1, s)]],
                                     incs=([(s_mm, 1)] if s == 1 else []))
                            else:
                                p.op("tensor", lambda e, s=s: e.matmul(
                                        u_ps[s][:, 0:256], wkm3[s][:],
                                        Usb[s][:],
                                        start=False, stop=True,
                                        skip_group_check=True),
                                     waits=[ext_sigs[(2, s)]],
                                     incs=([(s_mm, 1)] if s == 1 else []))
                        mmc["n"] += 1
                        lo, hi = EXT_SLICES[a]
                        for s in range(2):
                            ext_sigs[(a, s)] = p.op("vector",
                                 lambda e, s=s, g=g, lo=lo, hi=hi:
                                 e.tensor_scalar(
                                    Usb[s][lo:hi, :], u_ps[s][lo:hi, 0:256],
                                    wlg[s][lo:hi, g:g + 1], None, ALU.mult),
                                 waits=([(s_mm, mmc["n"])] if s == 0 else []))
                    uprime_done = ext_sigs[(3, 1)]
                    for s in range(2):
                        for cc in range(2):
                            p.op("tensor", lambda e, s=s, cc=cc, g=g: e.matmul(
                                    o_ps[:, 0:256],
                                    q_fm[:, cc * L4L + 128 * g:
                                         cc * L4L + 128 * (g + 1)],
                                    S_bf[s][:, cc * 256:(cc + 1) * 256],
                                    start=(s == 0 and cc == 0), stop=False),
                                 waits=([o_ev[0], o_ev[1]]
                                        if (s == 0 and cc == 0) else []))
                    for cc in range(2):
                        p.op("tensor", lambda e, cc=cc, g=g: e.matmul(
                                gram[:, 0:128],
                                k_fm[:, cc * L4L + 128 * g:
                                     cc * L4L + 128 * (g + 1)],
                                q_fm[:, cc * L4L + 128 * g:
                                     cc * L4L + 128 * (g + 1)],
                                start=(cc == 0), stop=(cc == 1)),
                             waits=([wm_free] if cc == 0 else []),
                             incs=([(s_mm, 1)] if cc == 1 else []))
                    mmc["n"] += 1
                    gr_sig = p.op("vector", lambda e: e.tensor_mul(
                            grm_sb[:], gram[:, 0:128], MS_CA),
                         waits=[(s_mm, mmc["n"])])
                    gram_free = gr_sig
                    for s in range(2):
                        p.op("tensor", lambda e, s=s: e.matmul(
                                o_ps[:, 0:256], grm_sb[:], Usb[s][:],
                                start=False, stop=(s == 1)),
                             waits=([gr_sig, uprime_done] if s == 0 else []),
                             incs=([(s_mm, 1)] if s == 1 else []))
                    mmc["n"] += 1
                    OMM = mmc["n"]
                    for s in range(2):
                        for cc in range(2):
                            p.op("tensor", lambda e, s=s, cc=cc, g=g: e.matmul(
                                    sd_ps[s][:, cc * 256:(cc + 1) * 256],
                                    ktm[:, 256 * g + 128 * cc:
                                        256 * g + 128 * (cc + 1)],
                                    Usb[s][:],
                                    start=(cc == 0), stop=(cc == 1)),
                                 waits=([sbf_sig[s]] if cc == 0 else []),
                                 incs=([(s_mm, 1)] if (s == 1 and cc == 1)
                                       else []))
                    mmc["n"] += 1
                    SMM = mmc["n"]
                    par = g % 2
                    oev = p.op("vector", lambda e, par=par: e.tensor_copy(
                            o_stage[:, par * 256:(par + 1) * 256],
                            o_ps[:, 0:256]),
                         waits=[(s_mm, OMM), (s_st[par], 16 * nstp[par])])
                    o_ev[par] = oev
                    p.op("sync", lambda e, hh=hh, g=g, par=par: e.dma_start(
                            out=o_out[(hh * NG + g) * 128:
                                      (hh * NG + g + 1) * 128, :],
                            in_=o_stage[:, par * 256:(par + 1) * 256]),
                         waits=[oev], incs=[(s_st[par], 16)])
                    nstp[par] += 1
                    for s in range(2):
                        p.op("vector", lambda e, s=s: e.tensor_add(
                                S_sb[s][:], S_sb[s][:], sd_ps[s][:, 0:512]),
                             waits=([(s_mm, SMM)] if s == 0 else ()))
                        sbf_sig[s] = p.op("vector", lambda e, s=s: e.tensor_copy(
                                S_bf[s][:], S_sb[s][:]))
                    vk_free = (s_mm, SMM)
                head_free = sbf_sig[1]
        p.op("sync", lambda e: e.nop(),
             waits=[(s_st[0], nstp[0] * 16), (s_st[1], nstp[1] * 16)])
        p.emit(c.s_const, c.ready_val)
    return nc


def run_l1scan(q_silu, k_silu, v_silu, beta, g_raw, hg_raw, bg, bh):
    import ml_dtypes
    nc = build_l1scan()
    f32 = lambda a: np.asarray(a, np.float32)
    NG = 32

    def tmify(a):
        return np.ascontiguousarray(
            a.reshape(NG, 128, 256).transpose(1, 0, 2).reshape(128, NG * 256))

    def colify(a):
        return np.ascontiguousarray(a.reshape(NG, 128).T)

    msk = np.zeros((128, 416), np.float32)
    r = np.arange(128)
    blk = r // 32
    for i in range(128):
        for jj in range(128):
            if blk[i] == blk[jj] and (i % 32) > (jj % 32):
                msk[i, 32 + jj] = 1.0
            if blk[i] < blk[jj]:
                msk[i, 160 + jj] = 1.0
            if blk[i] < blk[jj] or (blk[i] == blk[jj] and i <= jj):
                msk[i, 288 + jj] = 1.0
    in_maps = []
    for i in range(NCORES):
        j, pp = i // 2, i % 2
        m = {"msk": msk}
        bghv = np.zeros((128, 4), np.float32)
        for hh in range(2):
            h = 2 * pp + hh
            m[f"qtm{hh}"] = tmify(f32(q_silu)[j, :, h, :]).astype(
                ml_dtypes.bfloat16)
            m[f"ktm{hh}"] = tmify(f32(k_silu)[j, :, h, :]).astype(
                ml_dtypes.bfloat16)
            m[f"vtm{hh}"] = tmify(f32(v_silu)[j, :, h, :]).astype(
                ml_dtypes.bfloat16)
            m[f"bl{hh}"] = colify(f32(beta)[j, :, h])
            m[f"bg{hh}"] = colify(f32(beta)[j, :, 4 + h])
            m[f"gg{hh}"] = colify(f32(g_raw)[j, :, h])
            m[f"hh{hh}"] = colify(f32(hg_raw)[j, :, h])
            bghv[:, 2 * hh] = float(f32(bg)[h])
            bghv[:, 2 * hh + 1] = float(f32(bh)[h])
        m["bgh"] = bghv
        in_maps.append(m)
    res = run_bass_kernel_spmd(nc, in_maps, core_ids=list(range(NCORES)))
    o = np.zeros((4, 4096, 4, 256), np.float32)
    for i in range(NCORES):
        j, pp = i // 2, i % 2
        oo = res.results[i]["o_out"].reshape(2, NG * 128, 256)
        for hh in range(2):
            o[j, :, 2 * pp + hh, :] = oo[hh]
    return o


def kernel(L0_attn_x, L0_attn_Wq, L0_attn_Wk, L0_attn_Wv, L0_attn_Wo,
           L0_attn_last_k_init, L0_attn_last_v_init,
           L0_ffn_W1, L0_ffn_b1, L0_ffn_A1, L0_ffn_B1, L0_ffn_R1,
           L0_ffn_W2, L0_ffn_b2, L0_ffn_A2, L0_ffn_B2, L0_ffn_R2,
           L1_attn_Wq, L1_attn_Wk, L1_attn_Wv, L1_attn_conv_q, L1_attn_conv_k,
           L1_attn_conv_v, L1_attn_Wb, L1_attn_Wg, L1_attn_bg, L1_attn_Wh,
           L1_attn_bh, L1_attn_norm_w, L1_attn_Wo,
           L1_ffn_w1, L1_ffn_b1, L1_ffn_w2, L1_ffn_b2):
    x1 = run_l0attn(L0_attn_x, L0_attn_Wq, L0_attn_Wk, L0_attn_Wv, L0_attn_Wo,
                    L0_attn_last_k_init, L0_attn_last_v_init)
    x2 = run_l0ffn(x1, L0_ffn_W1, L0_ffn_b1, L0_ffn_A1, L0_ffn_B1, L0_ffn_R1,
                   L0_ffn_W2, L0_ffn_b2, L0_ffn_A2, L0_ffn_B2, L0_ffn_R2)
    qc, kc, vc, bsg, gout, hout = run_l1front(
        x2, L1_attn_Wq, L1_attn_Wk, L1_attn_Wv, L1_attn_conv_q, L1_attn_conv_k,
        L1_attn_conv_v, L1_attn_Wb, L1_attn_Wg, L1_attn_Wh)
    asm = lambda a: np.asarray(a, np.float32).transpose(0, 2, 1).reshape(
        4, 4096, 1024)
    q_full = asm(qc).reshape(4, 4096, 4, 256)
    k_full = asm(kc).reshape(4, 4096, 4, 256)
    v_full = asm(vc)
    beta = np.asarray(bsg, np.float32).transpose(0, 2, 1).reshape(4, 4096, 8)
    g_raw = np.asarray(gout, np.float32).transpose(0, 2, 1).reshape(4, 4096, 4)
    hg_raw = np.asarray(hout, np.float32).transpose(0, 2, 1).reshape(4, 4096, 4)
    o = run_l1scan(q_full, k_full, v_full.reshape(4, 4096, 4, 256), beta,
                   g_raw, hg_raw, L1_attn_bg, L1_attn_bh)
    x3 = run_l1comb(o.reshape(4, 4096, 1024), v_full, g_raw, hg_raw,
                    L1_attn_bg, L1_attn_bh, L1_attn_norm_w, L1_attn_Wo, x2)
    x4 = run_l1ffn(x3, L1_ffn_w1, L1_ffn_b1, L1_ffn_w2, L1_ffn_b2)
    return x4.astype(np.float32)
